# version 07
# baseline (speedup 1.0000x reference)
import contextlib
import numpy as np
import ml_dtypes
import concourse.bass as bass
import concourse.mybir as mybir
from concourse.bass_utils import run_bass_kernel_spmd

F32 = mybir.dt.float32
BF16 = mybir.dt.bfloat16
AF = mybir.ActivationFunctionType
ALU = mybir.AluOpType
AX = mybir.AxisListType

S = 4096
DM = 1024
DI = 2048
DS = 64
FF = 4096
NE = 4
KC = DM // 128
MCI = DI // 128
TP = 256
NSUB = S // TP
PB = 128
DBG_LN = False
DBG_P2 = True
NPB = S // PB

ENGINES = ("sync", "pe", "dve", "act", "pool")
BLKNAME = {"sync": "sync", "pe": "tensor", "dve": "vector", "act": "scalar",
           "pool": "gpsimd"}


class Sched:

    def __init__(self, serial=False):
        self.cnt = {e: 0 for e in ENGINES}
        self.water = {e: {} for e in ENGINES}
        self.serial = serial
        self._last = None
        self._seg = None
        self.sems = None
        self.barrier = set()
        self.streams = []
        self.last_write = {}
        self.readers = {}
        self.ro = set()
        self.mystart = {}

    def begin(self):
        self._seg = {e: [] for e in ENGINES}

    def _add(self, eng, key, fn, deps, inc):
        deps = [d for d in deps if d is not None]
        if self.serial and self._last is not None:
            deps = deps + [self._last]
        self._seg[eng].append((fn, deps, key, inc))
        self.cnt[key] = self.cnt.get(key, 0) + inc
        lab = (key, self.cnt[key])
        if inc > 0:
            self._last = lab
        return lab

    def _emit(self, eng, key, fn, reads, writes, deps, inc):
        deps = list(deps)
        for r in reads:
            w = self.last_write.get(r)
            if w is not None:
                deps.append(w)
        for wk in writes:
            w = self.last_write.get(wk)
            if w is not None:
                deps.append(w)
            deps.extend(self.readers.get(wk, ()))
        lab = self._add(eng, key, fn, deps, inc)
        for r in reads:
            if r not in self.ro:
                self.readers.setdefault(r, []).append(lab)
        for wk in writes:
            self.last_write[wk] = lab
            self.readers[wk] = []
        return lab

    def op(self, eng, fn, reads=(), writes=(), deps=()):
        return self._emit(eng, eng, fn, reads, writes, deps, 1)

    def op2(self, eng, fn, reads=(), writes=(), deps=()):
        self._emit(eng, eng, fn, reads, writes, deps, 1)
        return self._emit(eng, eng, fn, reads, writes, deps, 1)

    def dma(self, eng, fn, reads=(), writes=(), deps=(), stream="misc",
            barrier=False):
        key = "dma:" + stream
        if key not in self.cnt:
            self.streams.append(key)
            if barrier:
                self.barrier.add(key)
            self.sems[key] = self.es.enter_context(
                self.nc.semaphore("sem_" + stream))
        return self._emit(eng, key, fn, reads, writes, deps, 16)

    def fence(self):
        cuts = {k: self.cnt[k] for k in list(self.cnt)}
        for e in ENGINES:
            deps = [(k, c) for k, c in cuts.items() if k != e and c > 0]
            self._seg[e].append((None, deps, e, 0))

    def flush(self, blk, final=False):
        seg = self._seg
        self._seg = None
        for ename in ENGINES:
            ops = seg[ename]
            if not ops and not (final and ename == "sync"):
                continue

            def body(eng, ename=ename, ops=ops):
                water = self.water[ename]
                mycnt = self.mystart.get(ename, 0)
                for fn, deps, key, inc in ops:
                    best = {}
                    for (k2, c2) in deps:
                        if k2 == ename:
                            if ename != "pe" and c2 > mycnt - 8:
                                if c2 > best.get(k2, 0):
                                    best[k2] = c2
                            continue
                        if k2 in self.barrier:
                            c2 = self.cnt[k2]
                        if c2 > best.get(k2, 0):
                            best[k2] = c2
                    for k2, c2 in best.items():
                        if c2 > water.get(k2, 0):
                            eng.wait_ge(self.sems[k2], c2)
                            water[k2] = c2
                    if fn is None:
                        continue
                    inst = fn(eng)
                    if inc > 0:
                        inst.then_inc(self.sems[key], inc)
                        if key == ename:
                            mycnt += 1
                self.mystart[ename] = mycnt
                if final and ename == "sync":
                    for k in self.streams:
                        eng.wait_ge(self.sems[k], self.cnt[k])
            getattr(blk, BLKNAME[ename])(body)


def build_program(serial=False, debug=False):
    nc = bass.Bass("TRN2", target_bir_lowering=False)
    bfnp = ml_dtypes.bfloat16

    def din(name, shape):
        return nc.dram_tensor(name, shape, F32, kind="ExternalInput").ap()

    x_d = din("x", [S, DM])
    win_d = din("w_in", [DM, 2 * DI])
    wintl_d = din("w_intl", [DI, DM])
    wxp_d = din("w_xp", [DI, 129])
    wb_d = din("w_b", [DS, DI])
    wc_d = din("w_c", [DS, DI])
    wout_d = din("w_out", [DI, DM])
    dvec_d = din("dvec", [1, DI])
    alog_d = din("a_log", [1, 1])
    wgt_d = din("wg_t", [DM, NE])
    bg_d = din("bg", [1, NE])
    wit_d = din("wi_t", [DM, FF])
    wot_d = din("wo_t", [FF, DM])
    apad_d = din("a_pad", [DM, 128])
    bstk_d = din("b_stk", [128, FF])
    out_d = nc.dram_tensor("out", [S, DM], F32, kind="ExternalOutput").ap()

    scrkind = "ExternalOutput" if debug else "Internal"
    if debug:
        xnbf_dbg_d = nc.dram_tensor("xnbf_dbg", [128, 2 * DM], F32, kind="ExternalOutput").ap()
        rstd_dbg_d = nc.dram_tensor("rstd_dbg", [128, 2 * NPB], F32, kind="ExternalOutput").ap()
        gdbg_d = nc.dram_tensor("gdbg", [128, 32], F32, kind="ExternalOutput").ap()
        ln_dbg_d = nc.dram_tensor("ln_dbg", [128, 4 * NPB], F32, kind="ExternalOutput").ap()
        p2_d = nc.dram_tensor("p2dbg", [128, 16384], F32, kind="ExternalOutput").ap()
    xnT_d = nc.dram_tensor("xnT", [DM, S], BF16, kind=scrkind).ap()
    sT_d = nc.dram_tensor("sT", [DM, S], F32, kind=scrkind).ap()
    llT_d = nc.dram_tensor("llTs", [128, S], BF16, kind=scrkind).ap()
    a_scr_d = nc.dram_tensor("a_scr", [1, S], F32, kind=scrkind).ap()
    w_scr_d = nc.dram_tensor("w_scr", [NE, S], F32, kind=scrkind).ap()

    idb_d = nc.inline_tensor(np.eye(128, dtype=bfnp), name="c_idb").ap()
    idf_d = nc.inline_tensor(np.eye(128, dtype=np.float32), name="c_idf").ap()
    onesf_d = nc.inline_tensor(np.ones((1, 128), np.float32), name="c_onesf").ap()
    onesb_d = nc.inline_tensor(np.ones((1, 128), bfnp), name="c_onesb").ap()

    p = Sched(serial=serial)
    p.nc = nc

    def atl(setid):
        def f(e):
            return e.add_instruction(mybir.InstLoadActFuncSet(
                name=nc.get_next_instruction_name(),
                act_func_set_id=setid, ins=[], outs=[]))
        return f
    p.ro.update(["idb", "idf", "onesf", "onesb", "W_in", "W_out", "WBC",
                 "WproBC", "grhs", "Apad", "WinTl", "Wxp", "WiT", "WoT",
                 "Bstk", "dcol", "acol", "bgrep", "x_d"])

    es = contextlib.ExitStack()
    with es:
        p.es = es
        sems = {e: es.enter_context(nc.semaphore("sem_" + e)) for e in ENGINES}
        p.sems = sems
        idb = es.enter_context(nc.sbuf_tensor("idb", [128, 128], BF16))
        idf = es.enter_context(nc.sbuf_tensor("idf", [128, 128], F32))
        onesf = es.enter_context(nc.sbuf_tensor("onesf", [1, 128], F32))
        onesb = es.enter_context(nc.sbuf_tensor("onesb", [1, 128], BF16))
        dcol = es.enter_context(nc.sbuf_tensor("dcol", [128, MCI], F32))
        acol = es.enter_context(nc.sbuf_tensor("acol", [128, 1], F32))
        bgrep = es.enter_context(nc.sbuf_tensor("bgrep", [128, NE], F32))
        PT = [es.enter_context(nc.psum_tensor(f"pt{i}", [128, 1024], F32))
              for i in range(4)]
        BK = lambda i, h: ("bank", i, h)

        p.begin()
        p.op("act", atl(6))
        p.dma("sync", lambda e: e.dma_start(out=idb[:], in_=idb_d), writes=["idb"], stream="cst", barrier=True)
        p.dma("sync", lambda e: e.dma_start(out=idf[:], in_=idf_d), writes=["idf"], stream="cst", barrier=True)
        p.dma("sync", lambda e: e.dma_start(out=onesf[:], in_=onesf_d), writes=["onesf"], stream="cst", barrier=True)
        p.dma("sync", lambda e: e.dma_start(out=onesb[:], in_=onesb_d), writes=["onesb"], stream="cst", barrier=True)

        with contextlib.ExitStack() as es01:
            W_in = es01.enter_context(nc.sbuf_tensor("W_in", [128, KC * 4096], BF16))
            W_out = es01.enter_context(nc.sbuf_tensor("W_out", [128, MCI * 1024], BF16))
            WBC = es01.enter_context(nc.sbuf_tensor("WBC", [128, DI], BF16))
            WproBC = es01.enter_context(nc.sbuf_tensor("WproBC", [128, KC * 128], BF16))
            grhs = es01.enter_context(nc.sbuf_tensor("grhs", [128, KC * 5], BF16))
            Apad = es01.enter_context(nc.sbuf_tensor("Apad", [128, KC * 128], BF16))

            p.dma("pool", lambda e: e.dma_start(
                out=W_in.rearrange("p (kc m) -> p kc m", m=4096), in_=win_d.rearrange("(kc p) m -> p kc m", p=128)),
                writes=["W_in"], stream="wini", barrier=True)
            p.dma("pool", lambda e: e.dma_start(
                out=W_out.rearrange("p (kc m) -> p kc m", m=1024), in_=wout_d.rearrange("(kc p) m -> p kc m", p=128)),
                writes=["W_out"], stream="wini", barrier=True)
            p.dma("pool", lambda e: e.dma_start(out=WBC[0:64, :], in_=wb_d),
                  writes=[("WBC", 0)], stream="wini", barrier=True)
            p.dma("pool", lambda e: e.dma_start(out=WBC[64:128, :], in_=wc_d),
                  writes=[("WBC", 1)], stream="wini", barrier=True)
            l_wgt = p.dma("pool", lambda e: e.dma_start(
                out=grhs.rearrange("p (kc n) -> p kc n", n=5)[:, :, 1:5],
                in_=wgt_d.rearrange("(kc p) e -> p kc e", p=128)), stream="wini", barrier=True)
            p.dma("pool", lambda e: e.dma_start(
                out=Apad.rearrange("p (kc m) -> p kc m", m=128), in_=apad_d.rearrange("(kc p) m -> p kc m", p=128)),
                writes=["Apad"], stream="wini", barrier=True)

            with contextlib.ExitStack() as es0, nc.Block() as blk0:
                WinTl = es0.enter_context(nc.sbuf_tensor("WinTl", [128, 16 * 1024], BF16))
                Wxp = es0.enter_context(nc.sbuf_tensor("Wxp", [128, 16 * 129], BF16))
                dflat = es0.enter_context(nc.sbuf_tensor("dflat", [1, DI], F32))
                alog = es0.enter_context(nc.sbuf_tensor("alog", [1, 1], F32))
                a1n = es0.enter_context(nc.sbuf_tensor("a1n", [1, 1], F32))
                bgs = es0.enter_context(nc.sbuf_tensor("bgs", [1, NE], F32))
                xbuf = [es0.enter_context(nc.sbuf_tensor(f"xbuf{i}", [128, DM], F32))
                        for i in range(2)]
                xnbf = es0.enter_context(nc.sbuf_tensor("xnbf", [128, DM], BF16))
                xnfm = es0.enter_context(nc.sbuf_tensor("xnfm", [128, DM], BF16))
                llbf = es0.enter_context(nc.sbuf_tensor("llbf", [128, 128], BF16))
                scrA = es0.enter_context(nc.sbuf_tensor("scrA", [128, DM], F32))
                scrB = es0.enter_context(nc.sbuf_tensor("scrB", [128, DM], F32))
                cols = {}
                for nm in ["mcol", "msum", "sqcol", "m2t", "negb", "lnv", "rstd", "df",
                           "rd", "ad", "ed", "ld_", "spd", "logad", "acb",
                           "mx1", "negmx", "sume", "rs1", "m1c", "m2c"]:
                    cols[nm] = es0.enter_context(nc.sbuf_tensor(nm, [128, 1], F32))
                q4 = {}
                for nm in ["lg4", "ex4", "sc4", "mask1", "s24", "keepm", "w4t"]:
                    q4[nm] = es0.enter_context(nc.sbuf_tensor(nm, [128, NE], F32))
                artmp = es0.enter_context(nc.sbuf_tensor("artmp", [1, 128], F32))
                wrtmp4 = es0.enter_context(nc.sbuf_tensor("wrtmp4", [4, 128], F32))

                p.dma("pool", lambda e: e.dma_start(
                    out=WinTl.rearrange("p (kc m) -> p kc m", m=1024), in_=wintl_d.rearrange("(kc p) m -> p kc m", p=128)),
                    writes=["WinTl"], stream="wini", barrier=True)
                p.dma("pool", lambda e: e.dma_start(
                    out=Wxp.rearrange("p (kc m) -> p kc m", m=129), in_=wxp_d.rearrange("(kc p) m -> p kc m", p=128)),
                    writes=["Wxp"], stream="wini", barrier=True)
                p.dma("sync", lambda e: e.dma_start(out=dflat[:], in_=dvec_d),
                      writes=["dflat"], stream="cst", barrier=True)
                p.dma("sync", lambda e: e.dma_start(out=alog[:], in_=alog_d),
                      writes=["alog"], stream="cst", barrier=True)
                p.dma("sync", lambda e: e.dma_start(out=bgs[:], in_=bg_d),
                      writes=["bgs"], stream="cst", barrier=True)

                p.op2("act", lambda e: e.activation(a1n[:], alog[:], AF.Exp),
                     reads=["alog"], writes=["a1n"])
                p.op("pe", lambda e: e.matmul(PT[2][:, 0:1], onesf[:], a1n[:],
                                              start=True, stop=True),
                     reads=["onesf", "a1n"], writes=[BK(2, 0)])
                p.op2("dve", lambda e: e.tensor_copy(acol[:], PT[2][:, 0:1]),
                     reads=[BK(2, 0)], writes=["acol"])
                p.op("pe", lambda e: e.matmul(PT[2][:, 512:516], onesf[:], bgs[:],
                                              start=True, stop=True),
                     reads=["onesf", "bgs"], writes=[BK(2, 1)])
                p.op2("dve", lambda e: e.tensor_copy(bgrep[:], PT[2][:, 512:516]),
                     reads=[BK(2, 1)], writes=["bgrep"])
                for mc in range(MCI):
                    p.op("pe", lambda e, mc=mc: e.matmul(
                        PT[3][:, mc:mc + 1],
                        dflat[0:1, mc * 128:(mc + 1) * 128], onesf[0:1, 0:1],
                        start=True, stop=True),
                        reads=["dflat", "onesf"], writes=[BK(3, 0)])
                p.op2("dve", lambda e: e.tensor_copy(dcol[:], PT[3][:, 0:MCI]),
                     reads=[BK(3, 0)], writes=["dcol"])
                if debug:
                    p.dma("sync", lambda e: e.dma_start(
                        out=gdbg_d[:, 0:1], in_=acol[:],
                        allow_slow_non_contiguous=True), reads=["acol"], stream="dbg")
                for mc in range(KC):
                    lm = None
                    for kc2 in range(16):
                        lm = p.op("pe", lambda e, mc=mc, kc2=kc2: e.matmul(
                            PT[3][:, 512:641],
                            WinTl[:, kc2 * 1024 + mc * 128: kc2 * 1024 + (mc + 1) * 128],
                            Wxp[:, kc2 * 129:(kc2 + 1) * 129],
                            start=(kc2 == 0), stop=(kc2 == 15)),
                            reads=["WinTl", "Wxp"], writes=[BK(3, 1)])
                    p.op("dve", lambda e, mc=mc: e.tensor_copy(
                        grhs[:, mc * 5: mc * 5 + 1], PT[3][:, 512:513]),
                        reads=[BK(3, 1)], writes=[("grhs0", mc)], deps=[l_wgt])
                    p.op("dve", lambda e, mc=mc: e.tensor_copy(
                        WproBC[:, mc * 128:(mc + 1) * 128], PT[3][:, 513:641]),
                        reads=[BK(3, 1)], writes=[("WproBC", mc)])

                for blk in range(NPB):
                    par = blk % 2
                    xb = xbuf[par]
                    xkey = ("xbuf", par)
                    p.dma("sync", lambda e, blk=blk, xb=xb: e.dma_start(
                        out=xb[:], in_=x_d[blk * PB:(blk + 1) * PB, :]),
                        writes=[xkey], stream=f"xpro{par}")
                    p.op2("dve", lambda e, xb=xb: e.reduce_sum(
                        cols["msum"][:], xb[:], axis=AX.X),
                        reads=[xkey], writes=["msum"])
                    p.op2("dve", lambda e: e.tensor_scalar(
                        cols["mcol"][:], cols["msum"][:], 1.0 / DM, None,
                        op0=ALU.mult), reads=["msum"], writes=["mcol"])
                    p.op("act", lambda e, xb=xb: e.activation(
                        scrB[:], xb[:], AF.Square, accum_out=cols["sqcol"][:]),
                        reads=[xkey], writes=["scrB", "sqcol"])
                    p.op2("dve", lambda e: e.tensor_tensor(
                        cols["m2t"][:], cols["mcol"][:], cols["mcol"][:],
                        op=ALU.mult), reads=["mcol"], writes=["m2t"])
                    p.op2("dve", lambda e: e.tensor_scalar(
                        cols["negb"][:], cols["m2t"][:], -1.0, 1e-5,
                        op0=ALU.mult, op1=ALU.add), reads=["m2t"], writes=["negb"])
                    p.op2("act", lambda e: e.activation(
                        cols["lnv"][:], cols["sqcol"][:], AF.Ln,
                        bias=cols["negb"][:], scale=1.0 / DM),
                        reads=["sqcol", "negb"], writes=["lnv"])
                    p.op2("act", lambda e: e.activation(
                        cols["rstd"][:], cols["lnv"][:], AF.Exp, scale=-0.5),
                        reads=["lnv"], writes=["rstd"])
                    p.op("dve", lambda e, xb=xb: e.tensor_scalar(
                        xnbf[:], xb[:], cols["mcol"][:], cols["rstd"][:],
                        op0=ALU.subtract, op1=ALU.mult),
                        reads=[xkey, "mcol", "rstd"], writes=["xnbf"])
                    if debug and DBG_LN:
                        for ci, key in [(0, "sqcol"), (1, "negb"), (2, "lnv"),
                                        (3, "rstd")]:
                            p.dma("sync", lambda e, blk=blk, ci=ci, key=key:
                                  e.dma_start(
                                out=ln_dbg_d[:, 4 * blk + ci:4 * blk + ci + 1],
                                in_=cols[key][:],
                                allow_slow_non_contiguous=True),
                                reads=[key], stream="dbg")
                        p.dma("sync", lambda e, blk=blk: e.dma_start(
                            out=rstd_dbg_d[:, 2 * blk:2 * blk + 1],
                            in_=cols["rstd"][:],
                            allow_slow_non_contiguous=True), reads=["rstd"], stream="dbg")
                        p.dma("sync", lambda e, blk=blk: e.dma_start(
                            out=rstd_dbg_d[:, 2 * blk + 1:2 * blk + 2],
                            in_=cols["mcol"][:],
                            allow_slow_non_contiguous=True), reads=["mcol"], stream="dbg")
                        if blk < 2:
                            p.dma("gpsimd" if False else "pool",
                                  lambda e, blk=blk: e.dma_start(
                                out=xnbf_dbg_d[:, blk * DM:(blk + 1) * DM],
                                in_=xnbf[:]), reads=["xnbf"], stream="dbg")
                    for kc in range(KC):
                        p.op("pe", lambda e, kc=kc: e.matmul(
                            PT[0][:, kc * 128:(kc + 1) * 128],
                            xnbf[:, kc * 128:(kc + 1) * 128], idb[:],
                            start=True, stop=True),
                            reads=["xnbf", "idb"],
                            writes=[BK(0, 0) if kc < 4 else BK(0, 1)])
                    p.op("dve", lambda e: e.tensor_copy(xnfm[:], PT[0][:, 0:DM]),
                         reads=[BK(0, 0), BK(0, 1)], writes=["xnfm"])
                    p.dma("sync", lambda e, blk=blk: e.dma_start(
                        out=xnT_d.rearrange("(kc p) s -> p kc s", p=128)
                        [:, :, blk * PB:(blk + 1) * PB],
                        in_=xnfm.rearrange("p (kc t) -> p kc t", t=128)),
                        reads=["xnfm"], stream="xnTw")
                    for kc in range(KC):
                        p.op("pe", lambda e, kc=kc: e.matmul(
                            PT[1][:, 0:5], xnfm[:, kc * 128:(kc + 1) * 128],
                            grhs[:, kc * 5:(kc + 1) * 5],
                            start=(kc == 0), stop=(kc == KC - 1)),
                            reads=["xnfm"] + [("grhs0", kc2) for kc2 in range(KC)],
                            writes=[BK(1, 0)])
                    for kc in range(KC):
                        p.op("pe", lambda e, kc=kc: e.matmul(
                            PT[1][:, 512:640], Apad[:, kc * 128:(kc + 1) * 128],
                            xnfm[:, kc * 128:(kc + 1) * 128],
                            start=(kc == 0), stop=(kc == KC - 1)),
                            reads=["xnfm", "Apad"],
                            writes=[BK(1, 1)])
                    p.op("act", lambda e: e.activation(
                        llbf[:], PT[1][:, 512:640], AF.Identity),
                        reads=[BK(1, 1)], writes=["llbf"])
                    p.dma("sync", lambda e, blk=blk: e.dma_start(
                        out=llT_d[:, blk * PB:(blk + 1) * PB], in_=llbf[:]),
                        reads=["llbf"], stream="llw")
                    p.op2("dve", lambda e: e.tensor_copy(cols["df"][:], PT[1][:, 0:1]),
                         reads=[BK(1, 0)], writes=["df"])
                    p.op2("act", lambda e: e.activation(cols["rd"][:], cols["df"][:],
                                                       AF.Relu),
                         reads=["df"], writes=["rd"])
                    p.op2("act", lambda e: e.activation(cols["ad"][:], cols["df"][:],
                                                       AF.Abs),
                         reads=["df"], writes=["ad"])
                    p.op2("act", lambda e: e.activation(cols["ed"][:], cols["ad"][:],
                                                       AF.Exp, scale=-1.0),
                         reads=["ad"], writes=["ed"])
                    p.op2("act", lambda e: e.activation(cols["ld_"][:], cols["ed"][:],
                                                       AF.Ln, bias=1.0),
                         reads=["ed"], writes=["ld_"])
                    p.op2("dve", lambda e: e.tensor_tensor(
                        cols["spd"][:], cols["rd"][:], cols["ld_"][:], op=ALU.add),
                        reads=["rd", "ld_"], writes=["spd"])
                    p.op2("dve", lambda e: e.tensor_tensor(
                        cols["logad"][:], cols["spd"][:], acol[:], op=ALU.mult),
                        reads=["spd", "acol"], writes=["logad"])
                    p.op2("act", lambda e: e.activation(cols["acb"][:], cols["logad"][:],
                                                       AF.Exp, scale=-1.0),
                         reads=["logad"], writes=["acb"])
                    p.op("pe", lambda e: e.matmul(
                        PT[2][0:1, 0:128], cols["acb"][:], idf[:],
                        start=True, stop=True),
                        reads=["acb", "idf"], writes=[BK(2, 0)])
                    p.op2("dve", lambda e: e.tensor_copy(artmp[:], PT[2][0:1, 0:128]),
                         reads=[BK(2, 0)], writes=["artmp"])
                    p.dma("sync", lambda e, blk=blk: e.dma_start(
                        out=a_scr_d[:, blk * PB:(blk + 1) * PB], in_=artmp[:]),
                        reads=["artmp"], stream="aw")
                    p.op2("dve", lambda e: e.tensor_tensor(
                        q4["lg4"][:], PT[1][:, 1:5], bgrep[:], op=ALU.add),
                        reads=[BK(1, 0), "bgrep"], writes=["lg4"])
                    p.op2("dve", lambda e: e.reduce_max(cols["mx1"][:], q4["lg4"][:],
                                                       axis=AX.X),
                         reads=["lg4"], writes=["mx1"])
                    p.op2("dve", lambda e: e.tensor_scalar(
                        cols["negmx"][:], cols["mx1"][:], -1.0, None, op0=ALU.mult),
                        reads=["mx1"], writes=["negmx"])
                    p.op2("act", lambda e: e.activation(
                        q4["ex4"][:], q4["lg4"][:], AF.Exp, bias=cols["negmx"][:],
                        accum_out=cols["sume"][:]),
                        reads=["lg4", "negmx"], writes=["ex4", "sume"])
                    p.op2("dve", lambda e: e.reciprocal(cols["rs1"][:], cols["sume"][:]),
                         reads=["sume"], writes=["rs1"])
                    p.op2("dve", lambda e: e.tensor_scalar(
                        q4["sc4"][:], q4["ex4"][:], cols["rs1"][:], None,
                        op0=ALU.mult), reads=["ex4", "rs1"], writes=["sc4"])
                    p.op2("dve", lambda e: e.reduce_max(cols["m1c"][:], q4["sc4"][:],
                                                       axis=AX.X),
                         reads=["sc4"], writes=["m1c"])
                    p.op2("dve", lambda e: e.tensor_scalar(
                        q4["mask1"][:], q4["sc4"][:], cols["m1c"][:], None,
                        op0=ALU.is_ge), reads=["sc4", "m1c"], writes=["mask1"])
                    p.op2("dve", lambda e: e.scalar_tensor_tensor(
                        q4["s24"][:], q4["mask1"][:], -1e30, q4["sc4"][:],
                        op0=ALU.mult, op1=ALU.add),
                        reads=["mask1", "sc4"], writes=["s24"])
                    p.op2("dve", lambda e: e.reduce_max(cols["m2c"][:], q4["s24"][:],
                                                       axis=AX.X),
                         reads=["s24"], writes=["m2c"])
                    p.op2("dve", lambda e: e.tensor_scalar(
                        q4["keepm"][:], q4["sc4"][:], cols["m2c"][:], None,
                        op0=ALU.is_ge), reads=["sc4", "m2c"], writes=["keepm"])
                    p.op2("dve", lambda e: e.tensor_tensor(
                        q4["w4t"][:], q4["sc4"][:], q4["keepm"][:], op=ALU.mult),
                        reads=["sc4", "keepm"], writes=["w4t"])
                    if debug and blk == 0:
                        for ci, key in [(2, "lg4"), (7, "ex4"), (12, "sc4"),
                                        (16, "w4t"), (22, "mask1"), (26, "s24")]:
                            p.dma("sync", lambda e, ci=ci, key=key: e.dma_start(
                                out=gdbg_d[:, ci:ci + NE], in_=q4[key][:],
                                allow_slow_non_contiguous=True), reads=[key], stream="dbg")
                        for ci, key in [(6, "mx1"), (11, "sume"), (20, "spd"),
                                        (21, "df"), (30, "m1c"), (31, "m2c")]:
                            p.dma("sync", lambda e, ci=ci, key=key: e.dma_start(
                                out=gdbg_d[:, ci:ci + 1], in_=cols[key][:],
                                allow_slow_non_contiguous=True), reads=[key], stream="dbg")
                    p.op("pe", lambda e: e.matmul(
                        PT[2][0:4, 512:640], q4["w4t"][:], idf[:],
                        start=True, stop=True),
                        reads=["w4t", "idf"], writes=[BK(2, 1)])
                    p.op2("dve", lambda e: e.tensor_copy(
                        wrtmp4[:], PT[2][0:4, 512:640]),
                        reads=[BK(2, 1)], writes=["wrtmp"])
                    p.dma("sync", lambda e, blk=blk: e.dma_start(
                        out=w_scr_d[:, blk * PB:(blk + 1) * PB],
                        in_=wrtmp4[:]), reads=["wrtmp"], stream="ww")
                p.fence()
                p.flush(blk0)

            p.begin()
            p.op("act", atl(18))
            with contextlib.ExitStack() as es1, nc.Block() as blk1:
                xnf = [es1.enter_context(nc.sbuf_tensor(f"xnf{i}", [128, KC * TP], BF16))
                       for i in range(2)]
                arow = [es1.enter_context(nc.sbuf_tensor(f"arow{i}", [1, TP], F32))
                        for i in range(2)]
                xzbf = es1.enter_context(nc.sbuf_tensor("xzbf", [128, 32 * TP], BF16))
                bcraw = es1.enter_context(nc.sbuf_tensor("bcraw", [128, TP], BF16))
                bbf = es1.enter_context(nc.sbuf_tensor("bbf", [128, MCI * TP], BF16))
                cbf = es1.enter_context(nc.sbuf_tensor("cbf", [128, MCI * TP], BF16))
                arep = es1.enter_context(nc.sbuf_tensor("arep", [128, TP], F32))
                hch = [es1.enter_context(nc.sbuf_tensor(f"hch{i}", [128, TP], F32))
                       for i in range(2)]
                hcar = es1.enter_context(nc.sbuf_tensor("hcar", [128, MCI], F32))
                t1b = [es1.enter_context(nc.sbuf_tensor(f"t1b{i}", [128, TP], F32))
                       for i in range(2)]
                ypre = es1.enter_context(nc.sbuf_tensor("ypre", [128, MCI * TP], F32))
                szbf = es1.enter_context(nc.sbuf_tensor("szbf", [128, MCI * TP], BF16))
                ybf = es1.enter_context(nc.sbuf_tensor("ybf", [128, MCI * TP], BF16))
                ssb = es1.enter_context(nc.sbuf_tensor("ssb", [128, KC * TP], F32))

                for sub in range(NSUB):
                    par = sub % 2
                    xnkey = ("xnf", par)
                    p.dma("sync", lambda e, sub=sub, par=par: e.dma_start(
                        out=xnf[par].rearrange("p (kc t) -> p kc t", t=TP),
                        in_=xnT_d.rearrange("(kc p) s -> p kc s", p=128)
                        [:, :, sub * TP:(sub + 1) * TP]), writes=[xnkey], stream=f"xn1_{par}")
                    p.dma("sync", lambda e, sub=sub, par=par: e.dma_start(
                        out=arow[par][:], in_=a_scr_d[:, sub * TP:(sub + 1) * TP]),
                        writes=[("arow", par)], stream=f"ar{par}")
                    for mcp in range(16):
                        pz = PT[0][:, 0:512] if mcp % 2 == 0 else PT[0][:, 512:1024]
                        bkey = BK(0, mcp % 2)
                        for j in range(2):
                            mc = 2 * mcp + j
                            for kc in range(KC):
                                p.op("pe", lambda e, mc=mc, kc=kc, pz=pz, j=j,
                                     par=par: e.matmul(
                                    pz[:, j * TP:(j + 1) * TP],
                                    W_in[:, kc * 4096 + mc * 128:
                                         kc * 4096 + (mc + 1) * 128],
                                    xnf[par][:, kc * TP:(kc + 1) * TP],
                                    start=(kc == 0), stop=(kc == KC - 1)),
                                    reads=["W_in", xnkey],
                                    writes=[bkey])
                        eng = "dve" if mcp % 2 == 0 else "act"
                        if eng == "dve":
                            p.op(eng, lambda e, mcp=mcp, pz=pz: e.tensor_copy(
                                xzbf[:, mcp * 512:(mcp + 1) * 512], pz[:]),
                                reads=[bkey], writes=[("xz", 2 * mcp), ("xz", 2 * mcp + 1)])
                        else:
                            p.op(eng, lambda e, mcp=mcp, pz=pz: e.activation(
                                xzbf[:, mcp * 512:(mcp + 1) * 512], pz[:],
                                AF.Identity),
                                reads=[bkey], writes=[("xz", 2 * mcp), ("xz", 2 * mcp + 1)])
                    for kc in range(KC):
                        p.op("pe", lambda e, kc=kc, par=par: e.matmul(
                            PT[1][:, 0:TP], WproBC[:, kc * 128:(kc + 1) * 128],
                            xnf[par][:, kc * TP:(kc + 1) * TP],
                            start=(kc == 0), stop=(kc == KC - 1)),
                            reads=[xnkey] + [("WproBC", k) for k in range(KC)],
                            writes=[BK(1, 0)])
                    p.op("dve", lambda e: e.tensor_copy(bcraw[:], PT[1][:, 0:TP]),
                         reads=[BK(1, 0)], writes=["bcraw"])
                    p.op("pe", lambda e, par=par: e.matmul(
                        PT[1][:, 0:TP], onesf[:], arow[par][:],
                        start=True, stop=True),
                        reads=["onesf", ("arow", par)], writes=[BK(1, 0)])
                    p.op("dve", lambda e: e.tensor_copy(arep[:], PT[1][:, 0:TP]),
                         reads=[BK(1, 0)], writes=["arep"])
                    brot = [(PT[1][:, 512:768], BK(1, 1)), (PT[2][:, 0:256], BK(2, 0)),
                            (PT[0][:, 0:256], BK(0, 0)), (PT[0][:, 512:768], BK(0, 1))]
                    crot = [(PT[2][:, 512:768], BK(2, 1)), (PT[3][:, 0:256], BK(3, 0)),
                            (PT[3][:, 512:768], BK(3, 1)), (PT[1][:, 0:256], BK(1, 0))]
                    for mc in range(MCI):
                        pb, bkey = brot[mc % 4]
                        p.op("pe", lambda e, mc=mc, pb=pb: e.matmul(
                            pb[:, 0:TP], WBC[0:64, mc * 128:(mc + 1) * 128],
                            bcraw[0:64, :], start=True, stop=True),
                            reads=[("WBC", 0), "bcraw"], writes=[bkey])
                        pc, ckey = crot[mc % 4]
                        p.op("pe", lambda e, mc=mc, pc=pc: e.matmul(
                            pc[:, 0:TP], WBC[64:128, mc * 128:(mc + 1) * 128],
                            bcraw[64:128, :], start=True, stop=True),
                            reads=[("WBC", 1), "bcraw"], writes=[ckey])
                        p.op("dve", lambda e, mc=mc, pb=pb: e.tensor_tensor(
                            bbf[:, mc * TP:(mc + 1) * TP], pb[:, 0:TP],
                            xzbf[:, mc * TP:(mc + 1) * TP], op=ALU.mult),
                            reads=[bkey, ("xz", mc)], writes=[("bbf", mc)])
                        p.op("act", lambda e, mc=mc, pc=pc: e.activation(
                            cbf[:, mc * TP:(mc + 1) * TP], pc[:, 0:TP],
                            AF.Identity), reads=[ckey], writes=[("cbf", mc)])
                    for mc in range(MCI):
                        hb = hch[mc % 2]
                        tb = t1b[mc % 2]
                        init = 0.0 if sub == 0 else hcar[:, mc:mc + 1]
                        p.op("dve", lambda e, mc=mc, hb=hb, init=init:
                             e.tensor_tensor_scan(
                            hb[:], arep[:], bbf[:, mc * TP:(mc + 1) * TP], init,
                            op0=ALU.mult, op1=ALU.add),
                            reads=["arep", ("bbf", mc), ("hcar", mc)],
                            writes=[("hch", mc % 2)])
                        p.op("dve", lambda e, mc=mc, hb=hb: e.tensor_copy(
                            hcar[:, mc:mc + 1], hb[:, TP - 1:TP]),
                            reads=[("hch", mc % 2)], writes=[("hcar", mc)])
                        p.op("pool", lambda e, mc=mc, hb=hb, tb=tb: e.tensor_tensor(
                            tb[:], cbf[:, mc * TP:(mc + 1) * TP], hb[:],
                            op=ALU.mult),
                            reads=[("cbf", mc), ("hch", mc % 2)],
                            writes=[("t1b", mc % 2)])
                        p.op("dve", lambda e, mc=mc, tb=tb: e.scalar_tensor_tensor(
                            ypre[:, mc * TP:(mc + 1) * TP],
                            xzbf[:, mc * TP:(mc + 1) * TP], dcol[:, mc:mc + 1],
                            tb[:], op0=ALU.mult, op1=ALU.add),
                            reads=[("xz", mc), "dcol", ("t1b", mc % 2)],
                            writes=[("ypre", mc)])
                    p.op("act", lambda e: e.activation(
                        szbf[:], xzbf[:, MCI * TP:32 * TP], AF.Silu),
                        reads=[("xz", mc) for mc in range(MCI, 32)],
                        writes=["szbf"])
                    p.op("dve", lambda e: e.tensor_tensor(
                        ybf[:], ypre[:], szbf[:], op=ALU.mult),
                        reads=[("ypre", mc) for mc in range(MCI)] + ["szbf"],
                        writes=["ybf"])
                    for mc2 in range(KC):
                        ps_, skey = crot[mc2 % 4]
                        for kc2 in range(MCI):
                            p.op("pe", lambda e, mc2=mc2, kc2=kc2, ps_=ps_:
                                 e.matmul(
                                ps_[:, 0:TP],
                                W_out[:, kc2 * 1024 + mc2 * 128:
                                      kc2 * 1024 + (mc2 + 1) * 128],
                                ybf[:, kc2 * TP:(kc2 + 1) * TP],
                                start=(kc2 == 0), stop=(kc2 == MCI - 1)),
                                reads=["W_out", "ybf"],
                                writes=[skey])
                        p.op("dve", lambda e, mc2=mc2, ps_=ps_: e.tensor_copy(
                            ssb[:, mc2 * TP:(mc2 + 1) * TP], ps_[:, 0:TP]),
                            reads=[skey], writes=[("ssb", mc2)])
                    p.dma("sync", lambda e, sub=sub: e.dma_start(
                        out=sT_d.rearrange("(mc p) s -> p mc s", p=128)
                        [:, :, sub * TP:(sub + 1) * TP],
                        in_=ssb.rearrange("p (mc t) -> p mc t", t=TP)),
                        reads=[("ssb", mc2) for mc2 in range(KC)], stream="sw")
                p.fence()
                p.flush(blk1)

        p.begin()
        with contextlib.ExitStack() as es2, nc.Block() as blk2:
            WiT = es2.enter_context(nc.sbuf_tensor("WiT", [128, KC * 4096], BF16))
            WoT = es2.enter_context(nc.sbuf_tensor("WoT", [128, 32 * 1024], BF16))
            Bstk = es2.enter_context(nc.sbuf_tensor("Bstk", [128, FF], BF16))
            xnf2 = [es2.enter_context(nc.sbuf_tensor(f"xnf2_{i}", [128, KC * TP], BF16))
                    for i in range(2)]
            llb = [es2.enter_context(nc.sbuf_tensor(f"llb{i}", [128, TP], BF16))
                   for i in range(2)]
            wrow = [es2.enter_context(nc.sbuf_tensor(f"wrow{i}", [1, NE * TP], BF16))
                    for i in range(2)]
            sin = es2.enter_context(nc.sbuf_tensor("sin", [128, KC * TP], F32))
            xin = es2.enter_context(nc.sbuf_tensor("xin", [128, 2 * DM], F32))
            hw = es2.enter_context(nc.sbuf_tensor("hw", [128, 32 * TP], BF16))
            basebf = es2.enter_context(nc.sbuf_tensor("basebf", [128, 2048], BF16))
            rbf = es2.enter_context(nc.sbuf_tensor("rbf", [128, 2048], BF16))
            vbf = es2.enter_context(nc.sbuf_tensor("vbf", [128, 1024], BF16))
            wrep = es2.enter_context(nc.sbuf_tensor("wrep", [128, NE * TP], BF16))
            sgbf = es2.enter_context(nc.sbuf_tensor("sgbf", [128, KC * TP], BF16))
            sgtm = es2.enter_context(nc.sbuf_tensor("sgtm", [128, 2 * DM], BF16))
            otm = es2.enter_context(nc.sbuf_tensor("otm", [128, DM], F32))
            flsh = es2.enter_context(nc.sbuf_tensor("flsh", [128, 1], F32))

            p.dma("pool", lambda e: e.dma_start(
                out=WiT.rearrange("p (kc m) -> p kc m", m=4096), in_=wit_d.rearrange("(kc p) m -> p kc m", p=128)),
                writes=["WiT"], stream="w2", barrier=True)
            p.dma("pool", lambda e: e.dma_start(
                out=WoT.rearrange("p (kc m) -> p kc m", m=1024), in_=wot_d.rearrange("(kc p) m -> p kc m", p=128)),
                writes=["WoT"], stream="w2", barrier=True)
            p.dma("pool", lambda e: e.dma_start(out=Bstk[:], in_=bstk_d),
                  writes=["Bstk"], stream="w2", barrier=True)

            for sub in range(NSUB):
                par = sub % 2
                xnkey = ("xnf2", par)
                p.dma("sync", lambda e, sub=sub, par=par: e.dma_start(
                    out=xnf2[par].rearrange("p (kc t) -> p kc t", t=TP),
                    in_=xnT_d.rearrange("(kc p) s -> p kc s", p=128)
                    [:, :, sub * TP:(sub + 1) * TP]), writes=[xnkey], stream="xn2")
                p.dma("sync", lambda e, sub=sub, par=par: e.dma_start(
                    out=llb[par][:], in_=llT_d[:, sub * TP:(sub + 1) * TP]),
                    writes=[("llb", par)], stream=f"llr{par}")
                p.dma("pool", lambda e, sub=sub, par=par: e.dma_start(
                    out=wrow[par].rearrange("o (e t) -> o e t", t=TP),
                    in_=w_scr_d[:, sub * TP:(sub + 1) * TP].unsqueeze(0)),
                    writes=[("wrow", par)], stream=f"wr{par}")
                p.dma("pool", lambda e, sub=sub: e.dma_start(
                    out=sin.rearrange("p (mc t) -> p mc t", t=TP),
                    in_=sT_d.rearrange("(mc p) s -> p mc s", p=128)
                    [:, :, sub * TP:(sub + 1) * TP]), writes=["sin"], stream="sir")
                p.dma("sync", lambda e, sub=sub: e.dma_start(
                    out=xin.rearrange("p (tb d) -> p tb d", d=DM),
                    in_=x_d[sub * TP:(sub + 1) * TP, :]
                    .rearrange("(tb p) d -> p tb d", p=128)), writes=["xin"], stream="xir")
                for ei in range(NE):
                    p.op("pe", lambda e, ei=ei, par=par: e.matmul(
                        PT[1][:, ei * TP:(ei + 1) * TP], onesb[:],
                        wrow[par][0:1, ei * TP:(ei + 1) * TP],
                        start=True, stop=True),
                        reads=["onesb", ("wrow", par)],
                        writes=[BK(1, 0) if ei < 2 else BK(1, 1)])
                p.op("dve", lambda e: e.tensor_copy(wrep[:], PT[1][:, 0:NE * TP]),
                     reads=[BK(1, 0), BK(1, 1)], writes=["wrep"])
                if debug and DBG_P2 and sub == 0:
                    p.dma("pool", lambda e: e.dma_start(
                        out=p2_d[:, 0:1024], in_=wrep[:]), reads=["wrep"],
                        stream="dbg")
                for gp in range(4):
                  for g in (2 * gp, 2 * gp + 1):
                    bi = 0 if g % 2 == 0 else 2
                    bb = basebf[:, (g % 2) * 1024:(g % 2 + 1) * 1024]
                    for fcq in range(4):
                        fc = g * 4 + fcq
                        for kc in range(KC):
                            p.op("pe", lambda e, fc=fc, fcq=fcq, kc=kc, par=par,
                                 bi=bi: e.matmul(
                                PT[bi][:, fcq * TP:(fcq + 1) * TP],
                                WiT[:, kc * 4096 + fc * 128:
                                    kc * 4096 + (fc + 1) * 128],
                                xnf2[par][:, kc * TP:(kc + 1) * TP],
                                start=(kc == 0), stop=(kc == KC - 1)),
                                reads=["WiT", xnkey],
                                writes=[BK(bi, 0) if fcq < 2 else BK(bi, 1)])
                    p.op("dve", lambda e, bi=bi, bb=bb: e.tensor_copy(bb, PT[bi][:]),
                         reads=[BK(bi, 0), BK(bi, 1)], writes=[("basebf", g % 2)])
                  for g in (2 * gp, 2 * gp + 1):
                    bb = basebf[:, (g % 2) * 1024:(g % 2 + 1) * 1024]
                    for ei in range(NE):
                        ui = 1 if ei % 2 == 0 else 3
                        tp = (96, 0) if ei == 3 else None
                        for fcq in range(4):
                            fc = g * 4 + fcq
                            p.op("pe", lambda e, ei=ei, fc=fc, fcq=fcq, par=par,
                                 tp=tp, ui=ui: e.matmul(
                                PT[ui][:, fcq * TP:(fcq + 1) * TP],
                                Bstk[32 * ei:32 * ei + 32, fc * 128:(fc + 1) * 128],
                                llb[par][32 * ei:32 * ei + 32, :],
                                start=True, stop=False, tile_position=tp),
                                reads=["Bstk", ("llb", par)],
                                writes=[BK(ui, 0) if fcq < 2 else BK(ui, 1)])
                            p.op("pe", lambda e, fcq=fcq, ui=ui, bb=bb: e.matmul(
                                PT[ui][:, fcq * TP:(fcq + 1) * TP], idb[:],
                                bb[:, fcq * TP:(fcq + 1) * TP],
                                start=False, stop=True),
                                reads=["idb", ("basebf", g % 2)],
                                writes=[BK(ui, 0) if fcq < 2 else BK(ui, 1)])
                        rv = rbf[:, (ei % 2) * 1024:(ei % 2 + 1) * 1024]
                        if ei % 2 == 0:
                            p.op("act", lambda e, ui=ui, rv=rv: e.activation(
                                rv, PT[ui][:], AF.Relu),
                                reads=[BK(ui, 0), BK(ui, 1)],
                                writes=[("rbf", ei % 2)])
                        else:
                            p.op("dve", lambda e, ui=ui, rv=rv: e.tensor_scalar(
                                rv, PT[ui][:], 0.0, None, op0=ALU.max),
                                reads=[BK(ui, 0), BK(ui, 1)],
                                writes=[("rbf", ei % 2)])
                        wrap = wrep[:, ei * TP:(ei + 1) * TP].unsqueeze(1)\
                            .broadcast_to([128, 4, TP])
                        if ei == 0:
                            p.op("dve", lambda e, g=g, wrap=wrap, rv=rv:
                                 e.tensor_tensor(
                                hw[:, g * 1024:(g + 1) * 1024]
                                .rearrange("p (q t) -> p q t", t=TP),
                                rv.rearrange("p (q t) -> p q t", t=TP), wrap,
                                op=ALU.mult),
                                reads=[("rbf", ei % 2), "wrep"], writes=[("hw", g)])
                        else:
                            p.op("dve", lambda e, wrap=wrap, rv=rv: e.tensor_tensor(
                                vbf.rearrange("p (q t) -> p q t", t=TP),
                                rv.rearrange("p (q t) -> p q t", t=TP), wrap,
                                op=ALU.mult),
                                reads=[("rbf", ei % 2), "wrep"], writes=["vbf"])
                            eng_acc = "pool" if ei == 2 else "dve"
                            p.op(eng_acc, lambda e, g=g: e.tensor_tensor(
                                hw[:, g * 1024:(g + 1) * 1024],
                                hw[:, g * 1024:(g + 1) * 1024], vbf[:],
                                op=ALU.add),
                                reads=[("hw", g), "vbf"], writes=[("hw", g)])
                for mch in range(2):
                    pps = PT[2] if mch == 0 else PT[3]
                    bks = [BK(2, 0), BK(2, 1)] if mch == 0 else [BK(3, 0), BK(3, 1)]
                    for mcq in range(4):
                        mc = mch * 4 + mcq
                        for fc in range(32):
                            p.op("pe", lambda e, mc=mc, mcq=mcq, fc=fc, pps=pps:
                                 e.matmul(
                                pps[:, mcq * TP:(mcq + 1) * TP],
                                WoT[:, fc * 1024 + mc * 128:
                                    fc * 1024 + (mc + 1) * 128],
                                hw[:, fc * TP:(fc + 1) * TP],
                                start=(fc == 0), stop=(fc == 31)),
                                reads=["WoT"] + ([("hw", gg) for gg in range(8)]
                                                 if fc in (0, 31) else []),
                                writes=[bks[0] if mcq < 2 else bks[1]])
                    p.op("act", lambda e, mch=mch, pps=pps: e.activation(
                        rbf[:, mch * 1024:(mch + 1) * 1024], pps[:], AF.Silu),
                        reads=bks, writes=[("rbf", mch)])
                if debug and DBG_P2 and sub == 0:
                    p.dma("pool", lambda e: e.dma_start(
                        out=p2_d[:, 2048:10240], in_=hw[:]),
                        reads=[("hw", g) for g in range(8)], stream="dbg")
                    p.dma("pool", lambda e: e.dma_start(
                        out=p2_d[:, 10240:12288], in_=gbf[:]),
                        reads=[("gbf", 0), ("gbf", 1)], stream="dbg")
                p.op("dve", lambda e: e.tensor_tensor(
                    sgbf[:], sin[:], rbf[:], op=ALU.mult),
                    reads=["sin", ("rbf", 0), ("rbf", 1)], writes=["sgbf"])
                if debug and DBG_P2 and sub == 0:
                    p.dma("pool", lambda e: e.dma_start(
                        out=p2_d[:, 12288:14336], in_=sgbf[:]),
                        reads=["sgbf"], stream="dbg")
                for mc in range(KC):
                    for tb in range(2):
                        p.op("pe", lambda e, mc=mc, tb=tb: e.matmul(
                            PT[tb][:, mc * 128:(mc + 1) * 128],
                            sgbf[:, mc * TP + tb * 128:mc * TP + (tb + 1) * 128],
                            idb[:], start=True, stop=True),
                            reads=["sgbf", "idb"],
                            writes=[BK(tb, 0) if mc < 4 else BK(tb, 1)])
                for tb in range(2):
                    p.op("dve", lambda e, tb=tb: e.tensor_copy(
                        sgtm[:, tb * DM:(tb + 1) * DM], PT[tb][:]),
                        reads=[BK(tb, 0), BK(tb, 1)], writes=[("sgtm", tb)])
                if debug and DBG_P2 and sub == 0:
                    p.dma("pool", lambda e: e.dma_start(
                        out=p2_d[:, 14336:16384], in_=sgtm[:]),
                        reads=[("sgtm", 0), ("sgtm", 1)], stream="dbg")
                for tb in range(2):
                    p.op("dve", lambda e, tb=tb: e.tensor_tensor(
                        otm[:], sgtm[:, tb * DM:(tb + 1) * DM],
                        xin[:, tb * DM:(tb + 1) * DM], op=ALU.add),
                        reads=[("sgtm", tb), "xin"], writes=["otm"])
                    p.dma("sync", lambda e, sub=sub, tb=tb: e.dma_start(
                        out=out_d[sub * TP + tb * 128:sub * TP + (tb + 1) * 128, :],
                        in_=otm[:]), reads=["otm"], stream="outw")
            p.fence()
            p.flush(blk2, final=True)
    return nc


_NC_CACHE = {}


def kernel(L0_attn_x, L0_attn_W_in, L0_attn_W_xp, L0_attn_W_B, L0_attn_W_C,
           L0_attn_W_out, L0_attn_D, L0_attn_A_log, L0_ffn_Wg, L0_ffn_bg,
           L0_ffn_Wi, L0_ffn_Wo, L0_ffn_A, L0_ffn_B):
    f32 = np.float32
    x = np.ascontiguousarray(np.asarray(L0_attn_x, f32))
    W_in = np.ascontiguousarray(np.asarray(L0_attn_W_in, f32))
    W_xp = np.ascontiguousarray(np.asarray(L0_attn_W_xp, f32))
    W_B = np.ascontiguousarray(np.asarray(L0_attn_W_B, f32))
    W_C = np.ascontiguousarray(np.asarray(L0_attn_W_C, f32))
    W_out = np.ascontiguousarray(np.asarray(L0_attn_W_out, f32))
    Dv = np.ascontiguousarray(np.asarray(L0_attn_D, f32).reshape(1, DI))
    A_log = np.ascontiguousarray(np.asarray(L0_attn_A_log, f32).reshape(1, 1))
    Wg = np.asarray(L0_ffn_Wg, f32)
    bg = np.ascontiguousarray(np.asarray(L0_ffn_bg, f32).reshape(1, NE))
    Wi = np.asarray(L0_ffn_Wi, f32)
    Wo = np.asarray(L0_ffn_Wo, f32)
    Am = np.asarray(L0_ffn_A, f32)
    Bm = np.asarray(L0_ffn_B, f32)

    W_inTl = np.ascontiguousarray(W_in[:, :DI].T)
    WgT = np.ascontiguousarray(Wg.T)
    WiT = np.ascontiguousarray(Wi.T)
    WoT = np.ascontiguousarray(Wo.T)
    A_pad = np.zeros((128, DM), f32)
    B_stk = np.zeros((128, FF), f32)
    for e in range(NE):
        for r in range(4):
            A_pad[32 * e + r] = Am[e, r]
            B_stk[32 * e + r] = Bm[e, :, r]
    A_padT = np.ascontiguousarray(A_pad.T)

    key = "v1"
    if key not in _NC_CACHE:
        _NC_CACHE[key] = build_program()
    nc = _NC_CACHE[key]

    shared = dict(w_in=W_in, w_intl=W_inTl, w_xp=W_xp, w_b=W_B, w_c=W_C,
                  w_out=W_out, dvec=Dv, a_log=A_log, wg_t=WgT, bg=bg,
                  wi_t=WiT, wo_t=WoT, a_pad=A_padT, b_stk=B_stk)
    in_maps = [dict(x=np.ascontiguousarray(x[c]), **shared) for c in range(8)]
    res = run_bass_kernel_spmd(nc, in_maps, core_ids=list(range(8)))
    return np.stack([res.results[c]["out"] for c in range(8)], axis=0)


# version 08
# speedup vs baseline: 1.1027x; 1.1027x over previous
import contextlib
import numpy as np
import ml_dtypes
import concourse.bass as bass
import concourse.mybir as mybir
from concourse.bass_utils import run_bass_kernel_spmd

F32 = mybir.dt.float32
BF16 = mybir.dt.bfloat16
AF = mybir.ActivationFunctionType
ALU = mybir.AluOpType
AX = mybir.AxisListType

S = 4096
DM = 1024
DI = 2048
DS = 64
FF = 4096
NE = 4
KC = DM // 128
MCI = DI // 128
TP = 256
NSUB = S // TP
PB = 128
DBG_LN = False
DBG_P2 = True
NPB = S // PB

ENGINES = ("sync", "pe", "dve", "act", "pool")
BLKNAME = {"sync": "sync", "pe": "tensor", "dve": "vector", "act": "scalar",
           "pool": "gpsimd"}


class Sched:

    def __init__(self, serial=False):
        self.cnt = {e: 0 for e in ENGINES}
        self.water = {e: {} for e in ENGINES}
        self.serial = serial
        self._last = None
        self._seg = None
        self.sems = None
        self.barrier = set()
        self.streams = []
        self.last_write = {}
        self.readers = {}
        self.ro = set()
        self.mystart = {}

    def begin(self):
        self._seg = {e: [] for e in ENGINES}

    def _add(self, eng, key, fn, deps, inc):
        deps = [d for d in deps if d is not None]
        if self.serial and self._last is not None:
            deps = deps + [self._last]
        self._seg[eng].append((fn, deps, key, inc))
        self.cnt[key] = self.cnt.get(key, 0) + inc
        lab = (key, self.cnt[key])
        if inc > 0:
            self._last = lab
        return lab

    def _emit(self, eng, key, fn, reads, writes, deps, inc):
        deps = list(deps)
        for r in reads:
            w = self.last_write.get(r)
            if w is not None:
                deps.append(w)
        for wk in writes:
            w = self.last_write.get(wk)
            if w is not None:
                deps.append(w)
            deps.extend(self.readers.get(wk, ()))
        lab = self._add(eng, key, fn, deps, inc)
        for r in reads:
            if r not in self.ro:
                self.readers.setdefault(r, []).append(lab)
        for wk in writes:
            self.last_write[wk] = lab
            self.readers[wk] = []
        return lab

    def op(self, eng, fn, reads=(), writes=(), deps=()):
        return self._emit(eng, eng, fn, reads, writes, deps, 1)

    def op2(self, eng, fn, reads=(), writes=(), deps=()):
        self._emit(eng, eng, fn, reads, writes, deps, 1)
        return self._emit(eng, eng, fn, reads, writes, deps, 1)

    def dma(self, eng, fn, reads=(), writes=(), deps=(), stream="misc",
            barrier=False):
        key = "dma:" + stream
        if key not in self.cnt:
            self.streams.append(key)
            if barrier:
                self.barrier.add(key)
            self.sems[key] = self.es.enter_context(
                self.nc.semaphore("sem_" + stream))
        return self._emit(eng, key, fn, reads, writes, deps, 16)

    def fence(self):
        cuts = {k: self.cnt[k] for k in list(self.cnt)}
        for e in ENGINES:
            deps = [(k, c) for k, c in cuts.items() if k != e and c > 0]
            self._seg[e].append((None, deps, e, 0))

    def flush(self, blk, final=False):
        seg = self._seg
        self._seg = None
        for ename in ENGINES:
            ops = seg[ename]
            if not ops and not (final and ename == "sync"):
                continue

            def body(eng, ename=ename, ops=ops):
                water = self.water[ename]
                mycnt = self.mystart.get(ename, 0)
                for fn, deps, key, inc in ops:
                    best = {}
                    for (k2, c2) in deps:
                        if k2 == ename:
                            if ename != "pe" and c2 > mycnt - 8:
                                if c2 > best.get(k2, 0):
                                    best[k2] = c2
                            continue
                        if k2 in self.barrier:
                            c2 = self.cnt[k2]
                        if c2 > best.get(k2, 0):
                            best[k2] = c2
                    for k2, c2 in best.items():
                        if c2 > water.get(k2, 0):
                            eng.wait_ge(self.sems[k2], c2)
                            water[k2] = c2
                    if fn is None:
                        continue
                    inst = fn(eng)
                    if inc > 0:
                        inst.then_inc(self.sems[key], inc)
                        if key == ename:
                            mycnt += 1
                self.mystart[ename] = mycnt
                if final and ename == "sync":
                    for k in self.streams:
                        eng.wait_ge(self.sems[k], self.cnt[k])
            getattr(blk, BLKNAME[ename])(body)


def build_program(serial=False, debug=False):
    nc = bass.Bass("TRN2", target_bir_lowering=False)
    bfnp = ml_dtypes.bfloat16

    def din(name, shape):
        return nc.dram_tensor(name, shape, F32, kind="ExternalInput").ap()

    x_d = din("x", [S, DM])
    win_d = din("w_in", [DM, 2 * DI])
    wintl_d = din("w_intl", [DI, DM])
    wxp_d = din("w_xp", [DI, 129])
    wb_d = din("w_b", [DS, DI])
    wc_d = din("w_c", [DS, DI])
    wout_d = din("w_out", [DI, DM])
    dvec_d = din("dvec", [1, DI])
    alog_d = din("a_log", [1, 1])
    wgt_d = din("wg_t", [DM, NE])
    bg_d = din("bg", [1, NE])
    wit_d = din("wi_t", [DM, FF])
    wot_d = din("wo_t", [FF, DM])
    apad_d = din("a_pad", [DM, 128])
    bstk_d = din("b_stk", [128, FF])
    out_d = nc.dram_tensor("out", [S, DM], F32, kind="ExternalOutput").ap()

    scrkind = "ExternalOutput" if debug else "Internal"
    if debug:
        xnbf_dbg_d = nc.dram_tensor("xnbf_dbg", [128, 2 * DM], F32, kind="ExternalOutput").ap()
        rstd_dbg_d = nc.dram_tensor("rstd_dbg", [128, 2 * NPB], F32, kind="ExternalOutput").ap()
        gdbg_d = nc.dram_tensor("gdbg", [128, 32], F32, kind="ExternalOutput").ap()
        ln_dbg_d = nc.dram_tensor("ln_dbg", [128, 4 * NPB], F32, kind="ExternalOutput").ap()
        p2_d = nc.dram_tensor("p2dbg", [128, 16384], F32, kind="ExternalOutput").ap()
    xnT_d = nc.dram_tensor("xnT", [DM, S], BF16, kind=scrkind).ap()
    sT_d = nc.dram_tensor("sT", [DM, S], F32, kind=scrkind).ap()
    llT_d = nc.dram_tensor("llTs", [128, S], BF16, kind=scrkind).ap()
    a_scr_d = nc.dram_tensor("a_scr", [1, S], F32, kind=scrkind).ap()
    w_scr_d = nc.dram_tensor("w_scr", [NE, S], F32, kind=scrkind).ap()

    idb_d = nc.inline_tensor(np.eye(128, dtype=bfnp), name="c_idb").ap()
    idf_d = nc.inline_tensor(np.eye(128, dtype=np.float32), name="c_idf").ap()
    onesf_d = nc.inline_tensor(np.ones((1, 128), np.float32), name="c_onesf").ap()
    onesb_d = nc.inline_tensor(np.ones((1, 128), bfnp), name="c_onesb").ap()

    p = Sched(serial=serial)
    p.nc = nc

    def atl(setid):
        def f(e):
            return e.add_instruction(mybir.InstLoadActFuncSet(
                name=nc.get_next_instruction_name(),
                act_func_set_id=setid, ins=[], outs=[]))
        return f
    p.ro.update(["idb", "idf", "onesf", "onesb", "W_in", "W_out", "WBC",
                 "WproBC", "grhs", "Apad", "WinTl", "Wxp", "WiT", "WoT",
                 "Bstk", "dcol", "acol", "bgrep", "x_d"])

    es = contextlib.ExitStack()
    with es:
        p.es = es
        sems = {e: es.enter_context(nc.semaphore("sem_" + e)) for e in ENGINES}
        p.sems = sems
        idb = es.enter_context(nc.sbuf_tensor("idb", [128, 128], BF16))
        idf = es.enter_context(nc.sbuf_tensor("idf", [128, 128], F32))
        onesf = es.enter_context(nc.sbuf_tensor("onesf", [1, 128], F32))
        onesb = es.enter_context(nc.sbuf_tensor("onesb", [1, 128], BF16))
        dcol = es.enter_context(nc.sbuf_tensor("dcol", [128, MCI], F32))
        acol = es.enter_context(nc.sbuf_tensor("acol", [128, 1], F32))
        bgrep = es.enter_context(nc.sbuf_tensor("bgrep", [128, NE], F32))
        PT = [es.enter_context(nc.psum_tensor(f"pt{i}", [128, 1024], F32))
              for i in range(4)]
        BK = lambda i, h: ("bank", i, h)

        p.begin()
        p.op("act", atl(6))
        p.dma("sync", lambda e: e.dma_start(out=idb[:], in_=idb_d), writes=["idb"], stream="cst", barrier=True)
        p.dma("sync", lambda e: e.dma_start(out=idf[:], in_=idf_d), writes=["idf"], stream="cst", barrier=True)
        p.dma("sync", lambda e: e.dma_start(out=onesf[:], in_=onesf_d), writes=["onesf"], stream="cst", barrier=True)
        p.dma("sync", lambda e: e.dma_start(out=onesb[:], in_=onesb_d), writes=["onesb"], stream="cst", barrier=True)

        with contextlib.ExitStack() as es01:
            W_in = es01.enter_context(nc.sbuf_tensor("W_in", [128, KC * 4096], BF16))
            W_out = es01.enter_context(nc.sbuf_tensor("W_out", [128, MCI * 1024], BF16))
            WBC = es01.enter_context(nc.sbuf_tensor("WBC", [128, DI], BF16))
            WproBC = es01.enter_context(nc.sbuf_tensor("WproBC", [128, KC * 128], BF16))
            grhs = es01.enter_context(nc.sbuf_tensor("grhs", [128, KC * 5], BF16))
            Apad = es01.enter_context(nc.sbuf_tensor("Apad", [128, KC * 128], BF16))

            p.dma("pool", lambda e: e.dma_start(
                out=W_in.rearrange("p (kc m) -> p kc m", m=4096), in_=win_d.rearrange("(kc p) m -> p kc m", p=128)),
                writes=["W_in"], stream="wini", barrier=True)
            p.dma("pool", lambda e: e.dma_start(
                out=W_out.rearrange("p (kc m) -> p kc m", m=1024), in_=wout_d.rearrange("(kc p) m -> p kc m", p=128)),
                writes=["W_out"], stream="wini", barrier=True)
            p.dma("pool", lambda e: e.dma_start(out=WBC[0:64, :], in_=wb_d),
                  writes=[("WBC", 0)], stream="wini", barrier=True)
            p.dma("pool", lambda e: e.dma_start(out=WBC[64:128, :], in_=wc_d),
                  writes=[("WBC", 1)], stream="wini", barrier=True)
            l_wgt = p.dma("pool", lambda e: e.dma_start(
                out=grhs.rearrange("p (kc n) -> p kc n", n=5)[:, :, 1:5],
                in_=wgt_d.rearrange("(kc p) e -> p kc e", p=128)), stream="wini", barrier=True)
            p.dma("pool", lambda e: e.dma_start(
                out=Apad.rearrange("p (kc m) -> p kc m", m=128), in_=apad_d.rearrange("(kc p) m -> p kc m", p=128)),
                writes=["Apad"], stream="wini", barrier=True)

            with contextlib.ExitStack() as es0, nc.Block() as blk0:
                WinTl = es0.enter_context(nc.sbuf_tensor("WinTl", [128, 16 * 1024], BF16))
                Wxp = es0.enter_context(nc.sbuf_tensor("Wxp", [128, 16 * 129], BF16))
                dflat = es0.enter_context(nc.sbuf_tensor("dflat", [1, DI], F32))
                alog = es0.enter_context(nc.sbuf_tensor("alog", [1, 1], F32))
                a1n = es0.enter_context(nc.sbuf_tensor("a1n", [1, 1], F32))
                bgs = es0.enter_context(nc.sbuf_tensor("bgs", [1, NE], F32))
                xbuf = [es0.enter_context(nc.sbuf_tensor(f"xbuf{i}", [128, DM], F32))
                        for i in range(2)]
                xnbf = es0.enter_context(nc.sbuf_tensor("xnbf", [128, DM], BF16))
                xnfm = es0.enter_context(nc.sbuf_tensor("xnfm", [128, DM], BF16))
                llbf = es0.enter_context(nc.sbuf_tensor("llbf", [128, 128], BF16))
                scrA = es0.enter_context(nc.sbuf_tensor("scrA", [128, DM], F32))
                scrB = es0.enter_context(nc.sbuf_tensor("scrB", [128, DM], F32))
                cols = {}
                for nm in ["mcol", "msum", "sqcol", "m2t", "negb", "lnv", "rstd", "df",
                           "rd", "ad", "ed", "ld_", "spd", "logad", "acb",
                           "mx1", "negmx", "sume", "rs1", "m1c", "m2c"]:
                    cols[nm] = es0.enter_context(nc.sbuf_tensor(nm, [128, 1], F32))
                q4 = {}
                for nm in ["lg4", "ex4", "sc4", "mask1", "s24", "keepm", "w4t"]:
                    q4[nm] = es0.enter_context(nc.sbuf_tensor(nm, [128, NE], F32))
                artmp = es0.enter_context(nc.sbuf_tensor("artmp", [1, 128], F32))
                wrtmp4 = es0.enter_context(nc.sbuf_tensor("wrtmp4", [4, 128], F32))
                dall = es0.enter_context(nc.sbuf_tensor("dall", [128, NPB], F32))
                lgall = es0.enter_context(nc.sbuf_tensor("lgall", [128, 4 * NPB], F32))
                rdall = es0.enter_context(nc.sbuf_tensor("rdall", [128, NPB], F32))
                adall = es0.enter_context(nc.sbuf_tensor("adall", [128, NPB], F32))
                eadall = es0.enter_context(nc.sbuf_tensor("eadall", [128, NPB], F32))
                ladall = es0.enter_context(nc.sbuf_tensor("ladall", [128, NPB], F32))
                spall = es0.enter_context(nc.sbuf_tensor("spall", [128, NPB], F32))
                logall = es0.enter_context(nc.sbuf_tensor("logall", [128, NPB], F32))
                aall = es0.enter_context(nc.sbuf_tensor("aall", [128, NPB], F32))
                aTall = es0.enter_context(nc.sbuf_tensor("aTall", [32, 128], F32))
                mxall = es0.enter_context(nc.sbuf_tensor("mxall", [128, NPB], F32))
                exs = es0.enter_context(nc.sbuf_tensor("exs", [128, 4 * NPB], F32))
                exall = es0.enter_context(nc.sbuf_tensor("exall", [128, 4 * NPB], F32))
                sumall = es0.enter_context(nc.sbuf_tensor("sumall", [128, NPB], F32))
                rsall = es0.enter_context(nc.sbuf_tensor("rsall", [128, NPB], F32))
                scall = es0.enter_context(nc.sbuf_tensor("scall", [128, 4 * NPB], F32))
                s24t = es0.enter_context(nc.sbuf_tensor("s24t", [128, 4 * NPB], F32))
                wTall = es0.enter_context(nc.sbuf_tensor("wTall", [128, 128], F32))

                p.dma("pool", lambda e: e.dma_start(
                    out=WinTl.rearrange("p (kc m) -> p kc m", m=1024), in_=wintl_d.rearrange("(kc p) m -> p kc m", p=128)),
                    writes=["WinTl"], stream="wini", barrier=True)
                p.dma("pool", lambda e: e.dma_start(
                    out=Wxp.rearrange("p (kc m) -> p kc m", m=129), in_=wxp_d.rearrange("(kc p) m -> p kc m", p=128)),
                    writes=["Wxp"], stream="wini", barrier=True)
                p.dma("sync", lambda e: e.dma_start(out=dflat[:], in_=dvec_d),
                      writes=["dflat"], stream="cst", barrier=True)
                p.dma("sync", lambda e: e.dma_start(out=alog[:], in_=alog_d),
                      writes=["alog"], stream="cst", barrier=True)
                p.dma("sync", lambda e: e.dma_start(out=bgs[:], in_=bg_d),
                      writes=["bgs"], stream="cst", barrier=True)

                p.op2("act", lambda e: e.activation(a1n[:], alog[:], AF.Exp),
                     reads=["alog"], writes=["a1n"])
                p.op("pe", lambda e: e.matmul(PT[2][:, 0:1], onesf[:], a1n[:],
                                              start=True, stop=True),
                     reads=["onesf", "a1n"], writes=[BK(2, 0)])
                p.op2("dve", lambda e: e.tensor_copy(acol[:], PT[2][:, 0:1]),
                     reads=[BK(2, 0)], writes=["acol"])
                p.op("pe", lambda e: e.matmul(PT[2][:, 512:516], onesf[:], bgs[:],
                                              start=True, stop=True),
                     reads=["onesf", "bgs"], writes=[BK(2, 1)])
                p.op2("dve", lambda e: e.tensor_copy(bgrep[:], PT[2][:, 512:516]),
                     reads=[BK(2, 1)], writes=["bgrep"])
                for mc in range(MCI):
                    p.op("pe", lambda e, mc=mc: e.matmul(
                        PT[3][:, mc:mc + 1],
                        dflat[0:1, mc * 128:(mc + 1) * 128], onesf[0:1, 0:1],
                        start=True, stop=True),
                        reads=["dflat", "onesf"], writes=[BK(3, 0)])
                p.op2("dve", lambda e: e.tensor_copy(dcol[:], PT[3][:, 0:MCI]),
                     reads=[BK(3, 0)], writes=["dcol"])
                if debug:
                    p.dma("sync", lambda e: e.dma_start(
                        out=gdbg_d[:, 0:1], in_=acol[:],
                        allow_slow_non_contiguous=True), reads=["acol"], stream="dbg")
                for mc in range(KC):
                    lm = None
                    for kc2 in range(16):
                        lm = p.op("pe", lambda e, mc=mc, kc2=kc2: e.matmul(
                            PT[3][:, 512:641],
                            WinTl[:, kc2 * 1024 + mc * 128: kc2 * 1024 + (mc + 1) * 128],
                            Wxp[:, kc2 * 129:(kc2 + 1) * 129],
                            start=(kc2 == 0), stop=(kc2 == 15)),
                            reads=["WinTl", "Wxp"], writes=[BK(3, 1)])
                    p.op("dve", lambda e, mc=mc: e.tensor_copy(
                        grhs[:, mc * 5: mc * 5 + 1], PT[3][:, 512:513]),
                        reads=[BK(3, 1)], writes=[("grhs0", mc)], deps=[l_wgt])
                    p.op("dve", lambda e, mc=mc: e.tensor_copy(
                        WproBC[:, mc * 128:(mc + 1) * 128], PT[3][:, 513:641]),
                        reads=[BK(3, 1)], writes=[("WproBC", mc)])

                for blk in range(NPB):
                    par = blk % 2
                    xb = xbuf[par]
                    xkey = ("xbuf", par)
                    p.dma("sync", lambda e, blk=blk, xb=xb: e.dma_start(
                        out=xb[:], in_=x_d[blk * PB:(blk + 1) * PB, :]),
                        writes=[xkey], stream=f"xpro{par}")
                    p.op2("dve", lambda e, xb=xb: e.reduce_sum(
                        cols["msum"][:], xb[:], axis=AX.X),
                        reads=[xkey], writes=["msum"])
                    p.op2("dve", lambda e: e.tensor_scalar(
                        cols["mcol"][:], cols["msum"][:], 1.0 / DM, None,
                        op0=ALU.mult), reads=["msum"], writes=["mcol"])
                    p.op("act", lambda e, xb=xb: e.activation(
                        scrB[:], xb[:], AF.Square, accum_out=cols["sqcol"][:]),
                        reads=[xkey], writes=["scrB", "sqcol"])
                    p.op2("dve", lambda e: e.tensor_tensor(
                        cols["m2t"][:], cols["mcol"][:], cols["mcol"][:],
                        op=ALU.mult), reads=["mcol"], writes=["m2t"])
                    p.op2("dve", lambda e: e.tensor_scalar(
                        cols["negb"][:], cols["m2t"][:], -1.0, 1e-5,
                        op0=ALU.mult, op1=ALU.add), reads=["m2t"], writes=["negb"])
                    p.op2("act", lambda e: e.activation(
                        cols["lnv"][:], cols["sqcol"][:], AF.Ln,
                        bias=cols["negb"][:], scale=1.0 / DM),
                        reads=["sqcol", "negb"], writes=["lnv"])
                    p.op2("act", lambda e: e.activation(
                        cols["rstd"][:], cols["lnv"][:], AF.Exp, scale=-0.5),
                        reads=["lnv"], writes=["rstd"])
                    p.op("dve", lambda e, xb=xb: e.tensor_scalar(
                        xnbf[:], xb[:], cols["mcol"][:], cols["rstd"][:],
                        op0=ALU.subtract, op1=ALU.mult),
                        reads=[xkey, "mcol", "rstd"], writes=["xnbf"])
                    if debug and DBG_LN:
                        for ci, key in [(0, "sqcol"), (1, "negb"), (2, "lnv"),
                                        (3, "rstd")]:
                            p.dma("sync", lambda e, blk=blk, ci=ci, key=key:
                                  e.dma_start(
                                out=ln_dbg_d[:, 4 * blk + ci:4 * blk + ci + 1],
                                in_=cols[key][:],
                                allow_slow_non_contiguous=True),
                                reads=[key], stream="dbg")
                        p.dma("sync", lambda e, blk=blk: e.dma_start(
                            out=rstd_dbg_d[:, 2 * blk:2 * blk + 1],
                            in_=cols["rstd"][:],
                            allow_slow_non_contiguous=True), reads=["rstd"], stream="dbg")
                        p.dma("sync", lambda e, blk=blk: e.dma_start(
                            out=rstd_dbg_d[:, 2 * blk + 1:2 * blk + 2],
                            in_=cols["mcol"][:],
                            allow_slow_non_contiguous=True), reads=["mcol"], stream="dbg")
                        if blk < 2:
                            p.dma("gpsimd" if False else "pool",
                                  lambda e, blk=blk: e.dma_start(
                                out=xnbf_dbg_d[:, blk * DM:(blk + 1) * DM],
                                in_=xnbf[:]), reads=["xnbf"], stream="dbg")
                    for kc in range(KC):
                        p.op("pe", lambda e, kc=kc: e.matmul(
                            PT[0][:, kc * 128:(kc + 1) * 128],
                            xnbf[:, kc * 128:(kc + 1) * 128], idb[:],
                            start=True, stop=True),
                            reads=["xnbf", "idb"],
                            writes=[BK(0, 0) if kc < 4 else BK(0, 1)])
                    p.op("dve", lambda e: e.tensor_copy(xnfm[:], PT[0][:, 0:DM]),
                         reads=[BK(0, 0), BK(0, 1)], writes=["xnfm"])
                    p.dma("sync", lambda e, blk=blk: e.dma_start(
                        out=xnT_d.rearrange("(kc p) s -> p kc s", p=128)
                        [:, :, blk * PB:(blk + 1) * PB],
                        in_=xnfm.rearrange("p (kc t) -> p kc t", t=128)),
                        reads=["xnfm"], stream="xnTw")
                    for kc in range(KC):
                        p.op("pe", lambda e, kc=kc: e.matmul(
                            PT[1][:, 0:5], xnfm[:, kc * 128:(kc + 1) * 128],
                            grhs[:, kc * 5:(kc + 1) * 5],
                            start=(kc == 0), stop=(kc == KC - 1)),
                            reads=["xnfm"] + [("grhs0", kc2) for kc2 in range(KC)],
                            writes=[BK(1, 0)])
                    for kc in range(KC):
                        p.op("pe", lambda e, kc=kc: e.matmul(
                            PT[1][:, 512:640], Apad[:, kc * 128:(kc + 1) * 128],
                            xnfm[:, kc * 128:(kc + 1) * 128],
                            start=(kc == 0), stop=(kc == KC - 1)),
                            reads=["xnfm", "Apad"],
                            writes=[BK(1, 1)])
                    p.op("act", lambda e: e.activation(
                        llbf[:], PT[1][:, 512:640], AF.Identity),
                        reads=[BK(1, 1)], writes=["llbf"])
                    p.dma("sync", lambda e, blk=blk: e.dma_start(
                        out=llT_d[:, blk * PB:(blk + 1) * PB], in_=llbf[:]),
                        reads=["llbf"], stream="llw")
                    p.op2("dve", lambda e, blk=blk: e.tensor_copy(
                        dall[:, blk:blk + 1], PT[1][:, 0:1]),
                        reads=[BK(1, 0)], writes=[("dall", blk)])
                    p.op2("dve", lambda e, blk=blk: e.tensor_tensor(
                        lgall[:, blk * 4:(blk + 1) * 4], PT[1][:, 1:5],
                        bgrep[:], op=ALU.add),
                        reads=[BK(1, 0), "bgrep"], writes=[("lgall", blk)])
                dR = [("dall", b) for b in range(NPB)]
                lR = [("lgall", b) for b in range(NPB)]
                p.op2("act", lambda e: e.activation(rdall[:], dall[:], AF.Relu),
                      reads=dR, writes=["rdall"])
                p.op2("act", lambda e: e.activation(adall[:], dall[:], AF.Abs),
                      reads=dR, writes=["adall"])
                p.op2("act", lambda e: e.activation(eadall[:], adall[:], AF.Exp,
                                                    scale=-1.0),
                      reads=["adall"], writes=["eadall"])
                p.op2("act", lambda e: e.activation(ladall[:], eadall[:], AF.Ln,
                                                    bias=1.0),
                      reads=["eadall"], writes=["ladall"])
                p.op2("dve", lambda e: e.tensor_tensor(
                    spall[:], rdall[:], ladall[:], op=ALU.add),
                    reads=["rdall", "ladall"], writes=["spall"])
                p.op2("dve", lambda e: e.tensor_scalar(
                    logall[:], spall[:], acol[:], None, op0=ALU.mult),
                    reads=["spall", "acol"], writes=["logall"])
                p.op2("act", lambda e: e.activation(aall[:], logall[:], AF.Exp,
                                                    scale=-1.0),
                      reads=["logall"], writes=["aall"])
                p.op("pe", lambda e: e.matmul(PT[2][0:32, 0:128], aall[:], idf[:],
                                              start=True, stop=True),
                     reads=["aall", "idf"], writes=[BK(2, 0)])
                p.op2("dve", lambda e: e.tensor_copy(aTall[:], PT[2][0:32, 0:128]),
                      reads=[BK(2, 0)], writes=["aTall"])
                p.dma("sync", lambda e: e.dma_start(
                    out=a_scr_d.rearrange("o (b t) -> b o t", t=128),
                    in_=aTall[:].unsqueeze(1)),
                    reads=["aTall"], stream="aw")
                lgv = lgall.rearrange("p (b e) -> p b e", e=4)
                def bc(t2d):
                    return t2d[:].unsqueeze(2).broadcast_to([128, NPB, 4])
                def v3(t):
                    return t.rearrange("p (b e) -> p b e", e=4)
                p.op2("dve", lambda e: e.reduce_max(
                    mxall[:].unsqueeze(2), lgv, axis=AX.X),
                    reads=lR, writes=["mxall"])
                p.op2("dve", lambda e: e.tensor_tensor(
                    v3(exs), lgv, bc(mxall), op=ALU.subtract),
                    reads=lR + ["mxall"], writes=["exsub"])
                p.op2("act", lambda e: e.activation(exall[:], exs[:], AF.Exp),
                      reads=["exsub"], writes=["exall"])
                p.op2("dve", lambda e: e.reduce_sum(
                    sumall[:].unsqueeze(2), v3(exall), axis=AX.X),
                    reads=["exall"], writes=["sumall"])
                p.op2("dve", lambda e: e.reciprocal(rsall[:], sumall[:]),
                      reads=["sumall"], writes=["rsall"])
                p.op2("dve", lambda e: e.tensor_tensor(
                    v3(scall), v3(exall), bc(rsall), op=ALU.mult),
                    reads=["exall", "rsall"], writes=["scall"])
                p.op2("dve", lambda e: e.reduce_max(
                    mxall[:].unsqueeze(2), v3(scall), axis=AX.X),
                    reads=["scall"], writes=["m1all"])
                p.op2("dve", lambda e: e.tensor_tensor(
                    v3(exs), v3(scall), bc(mxall), op=ALU.is_ge),
                    reads=["scall", "m1all"], writes=["mask1all"])
                p.op2("dve", lambda e: e.scalar_tensor_tensor(
                    s24t[:], exs[:], -1e30, scall[:],
                    op0=ALU.mult, op1=ALU.add),
                    reads=["mask1all", "scall"], writes=["s24all"])
                p.op2("dve", lambda e: e.reduce_max(
                    mxall[:].unsqueeze(2), v3(s24t), axis=AX.X),
                    reads=["s24all"], writes=["m2all"])
                p.op2("dve", lambda e: e.tensor_tensor(
                    v3(exs), v3(scall), bc(mxall), op=ALU.is_ge),
                    reads=["scall", "m2all"], writes=["keepall"])
                p.op2("dve", lambda e: e.tensor_tensor(
                    s24t[:], scall[:], exs[:], op=ALU.mult),
                    reads=["scall", "keepall"], writes=["w4tall"])
                p.op("pe", lambda e: e.matmul(PT[2][:, 512:640], s24t[:], idf[:],
                                              start=True, stop=True),
                     reads=["w4tall", "idf"], writes=[BK(2, 1)])
                p.op2("dve", lambda e: e.tensor_copy(wTall[:], PT[2][:, 512:640]),
                      reads=[BK(2, 1)], writes=["wTall"])
                for ei in range(NE):
                    p.dma("sync", lambda e, ei=ei: e.dma_start(
                        out=w_scr_d[ei:ei + 1, :].rearrange("o (b t) -> b o t", t=128),
                        in_=wTall[ei:128:4, :].unsqueeze(1)),
                        reads=["wTall"], stream="ww")
                p.fence()
                p.flush(blk0)

            p.begin()
            p.op("act", atl(18))
            with contextlib.ExitStack() as es1, nc.Block() as blk1:
                xnf = [es1.enter_context(nc.sbuf_tensor(f"xnf{i}", [128, KC * TP], BF16))
                       for i in range(2)]
                arow = [es1.enter_context(nc.sbuf_tensor(f"arow{i}", [1, TP], F32))
                        for i in range(2)]
                xzbf = es1.enter_context(nc.sbuf_tensor("xzbf", [128, 32 * TP], BF16))
                bcraw = es1.enter_context(nc.sbuf_tensor("bcraw", [128, TP], BF16))
                bbf = es1.enter_context(nc.sbuf_tensor("bbf", [128, MCI * TP], BF16))
                cbf = es1.enter_context(nc.sbuf_tensor("cbf", [128, MCI * TP], BF16))
                arep = es1.enter_context(nc.sbuf_tensor("arep", [128, TP], F32))
                hch = [es1.enter_context(nc.sbuf_tensor(f"hch{i}", [128, TP], F32))
                       for i in range(2)]
                hcar = es1.enter_context(nc.sbuf_tensor("hcar", [128, MCI], F32))
                t1b = [es1.enter_context(nc.sbuf_tensor(f"t1b{i}", [128, TP], F32))
                       for i in range(2)]
                ypre = es1.enter_context(nc.sbuf_tensor("ypre", [128, MCI * TP], F32))
                szbf = es1.enter_context(nc.sbuf_tensor("szbf", [128, MCI * TP], BF16))
                ybf = es1.enter_context(nc.sbuf_tensor("ybf", [128, MCI * TP], BF16))
                ssb = es1.enter_context(nc.sbuf_tensor("ssb", [128, KC * TP], F32))

                for sub in range(NSUB):
                    par = sub % 2
                    xnkey = ("xnf", par)
                    p.dma("sync", lambda e, sub=sub, par=par: e.dma_start(
                        out=xnf[par].rearrange("p (kc t) -> p kc t", t=TP),
                        in_=xnT_d.rearrange("(kc p) s -> p kc s", p=128)
                        [:, :, sub * TP:(sub + 1) * TP]), writes=[xnkey], stream=f"xn1_{par}")
                    p.dma("sync", lambda e, sub=sub, par=par: e.dma_start(
                        out=arow[par][:], in_=a_scr_d[:, sub * TP:(sub + 1) * TP]),
                        writes=[("arow", par)], stream=f"ar{par}")
                    for mcp in range(16):
                        pz = PT[0][:, 0:512] if mcp % 2 == 0 else PT[0][:, 512:1024]
                        bkey = BK(0, mcp % 2)
                        for j in range(2):
                            mc = 2 * mcp + j
                            for kc in range(KC):
                                p.op("pe", lambda e, mc=mc, kc=kc, pz=pz, j=j,
                                     par=par: e.matmul(
                                    pz[:, j * TP:(j + 1) * TP],
                                    W_in[:, kc * 4096 + mc * 128:
                                         kc * 4096 + (mc + 1) * 128],
                                    xnf[par][:, kc * TP:(kc + 1) * TP],
                                    start=(kc == 0), stop=(kc == KC - 1)),
                                    reads=["W_in", xnkey],
                                    writes=[bkey])
                        eng = "dve" if mcp % 2 == 0 else "act"
                        if eng == "dve":
                            p.op(eng, lambda e, mcp=mcp, pz=pz: e.tensor_copy(
                                xzbf[:, mcp * 512:(mcp + 1) * 512], pz[:]),
                                reads=[bkey], writes=[("xz", 2 * mcp), ("xz", 2 * mcp + 1)])
                        else:
                            p.op(eng, lambda e, mcp=mcp, pz=pz: e.activation(
                                xzbf[:, mcp * 512:(mcp + 1) * 512], pz[:],
                                AF.Identity),
                                reads=[bkey], writes=[("xz", 2 * mcp), ("xz", 2 * mcp + 1)])
                    for kc in range(KC):
                        p.op("pe", lambda e, kc=kc, par=par: e.matmul(
                            PT[1][:, 0:TP], WproBC[:, kc * 128:(kc + 1) * 128],
                            xnf[par][:, kc * TP:(kc + 1) * TP],
                            start=(kc == 0), stop=(kc == KC - 1)),
                            reads=[xnkey] + [("WproBC", k) for k in range(KC)],
                            writes=[BK(1, 0)])
                    p.op("dve", lambda e: e.tensor_copy(bcraw[:], PT[1][:, 0:TP]),
                         reads=[BK(1, 0)], writes=["bcraw"])
                    p.op("pe", lambda e, par=par: e.matmul(
                        PT[1][:, 0:TP], onesf[:], arow[par][:],
                        start=True, stop=True),
                        reads=["onesf", ("arow", par)], writes=[BK(1, 0)])
                    p.op("dve", lambda e: e.tensor_copy(arep[:], PT[1][:, 0:TP]),
                         reads=[BK(1, 0)], writes=["arep"])
                    brot = [(PT[1][:, 512:768], BK(1, 1)), (PT[2][:, 0:256], BK(2, 0)),
                            (PT[0][:, 0:256], BK(0, 0)), (PT[0][:, 512:768], BK(0, 1))]
                    crot = [(PT[2][:, 512:768], BK(2, 1)), (PT[3][:, 0:256], BK(3, 0)),
                            (PT[3][:, 512:768], BK(3, 1)), (PT[1][:, 0:256], BK(1, 0))]
                    for mc in range(MCI):
                        pb, bkey = brot[mc % 4]
                        p.op("pe", lambda e, mc=mc, pb=pb: e.matmul(
                            pb[:, 0:TP], WBC[0:64, mc * 128:(mc + 1) * 128],
                            bcraw[0:64, :], start=True, stop=True),
                            reads=[("WBC", 0), "bcraw"], writes=[bkey])
                        pc, ckey = crot[mc % 4]
                        p.op("pe", lambda e, mc=mc, pc=pc: e.matmul(
                            pc[:, 0:TP], WBC[64:128, mc * 128:(mc + 1) * 128],
                            bcraw[64:128, :], start=True, stop=True),
                            reads=[("WBC", 1), "bcraw"], writes=[ckey])
                        p.op("dve", lambda e, mc=mc, pb=pb: e.tensor_tensor(
                            bbf[:, mc * TP:(mc + 1) * TP], pb[:, 0:TP],
                            xzbf[:, mc * TP:(mc + 1) * TP], op=ALU.mult),
                            reads=[bkey, ("xz", mc)], writes=[("bbf", mc)])
                        p.op("act", lambda e, mc=mc, pc=pc: e.activation(
                            cbf[:, mc * TP:(mc + 1) * TP], pc[:, 0:TP],
                            AF.Identity), reads=[ckey], writes=[("cbf", mc)])
                    for mc in range(MCI):
                        hb = hch[mc % 2]
                        tb = t1b[mc % 2]
                        init = 0.0 if sub == 0 else hcar[:, mc:mc + 1]
                        p.op("dve", lambda e, mc=mc, hb=hb, init=init:
                             e.tensor_tensor_scan(
                            hb[:], arep[:], bbf[:, mc * TP:(mc + 1) * TP], init,
                            op0=ALU.mult, op1=ALU.add),
                            reads=["arep", ("bbf", mc), ("hcar", mc)],
                            writes=[("hch", mc % 2)])
                        p.op("dve", lambda e, mc=mc, hb=hb: e.tensor_copy(
                            hcar[:, mc:mc + 1], hb[:, TP - 1:TP]),
                            reads=[("hch", mc % 2)], writes=[("hcar", mc)])
                        p.op("dve", lambda e, mc=mc, hb=hb, tb=tb: e.tensor_tensor(
                            tb[:], cbf[:, mc * TP:(mc + 1) * TP], hb[:],
                            op=ALU.mult),
                            reads=[("cbf", mc), ("hch", mc % 2)],
                            writes=[("t1b", mc % 2)])
                        p.op("dve", lambda e, mc=mc, tb=tb: e.scalar_tensor_tensor(
                            ypre[:, mc * TP:(mc + 1) * TP],
                            xzbf[:, mc * TP:(mc + 1) * TP], dcol[:, mc:mc + 1],
                            tb[:], op0=ALU.mult, op1=ALU.add),
                            reads=[("xz", mc), "dcol", ("t1b", mc % 2)],
                            writes=[("ypre", mc)])
                    p.op("act", lambda e: e.activation(
                        szbf[:], xzbf[:, MCI * TP:32 * TP], AF.Silu),
                        reads=[("xz", mc) for mc in range(MCI, 32)],
                        writes=["szbf"])
                    p.op("dve", lambda e: e.tensor_tensor(
                        ybf[:], ypre[:], szbf[:], op=ALU.mult),
                        reads=[("ypre", mc) for mc in range(MCI)] + ["szbf"],
                        writes=["ybf"])
                    for mc2 in range(KC):
                        ps_, skey = crot[mc2 % 4]
                        for kc2 in range(MCI):
                            p.op("pe", lambda e, mc2=mc2, kc2=kc2, ps_=ps_:
                                 e.matmul(
                                ps_[:, 0:TP],
                                W_out[:, kc2 * 1024 + mc2 * 128:
                                      kc2 * 1024 + (mc2 + 1) * 128],
                                ybf[:, kc2 * TP:(kc2 + 1) * TP],
                                start=(kc2 == 0), stop=(kc2 == MCI - 1)),
                                reads=["W_out", "ybf"],
                                writes=[skey])
                        p.op("dve", lambda e, mc2=mc2, ps_=ps_: e.tensor_copy(
                            ssb[:, mc2 * TP:(mc2 + 1) * TP], ps_[:, 0:TP]),
                            reads=[skey], writes=[("ssb", mc2)])
                    p.dma("sync", lambda e, sub=sub: e.dma_start(
                        out=sT_d.rearrange("(mc p) s -> p mc s", p=128)
                        [:, :, sub * TP:(sub + 1) * TP],
                        in_=ssb.rearrange("p (mc t) -> p mc t", t=TP)),
                        reads=[("ssb", mc2) for mc2 in range(KC)], stream="sw")
                p.fence()
                p.flush(blk1)

        p.begin()
        with contextlib.ExitStack() as es2, nc.Block() as blk2:
            WiT = es2.enter_context(nc.sbuf_tensor("WiT", [128, KC * 4096], BF16))
            WoT = es2.enter_context(nc.sbuf_tensor("WoT", [128, 32 * 1024], BF16))
            Bstk = es2.enter_context(nc.sbuf_tensor("Bstk", [128, FF], BF16))
            xnf2 = [es2.enter_context(nc.sbuf_tensor(f"xnf2_{i}", [128, KC * TP], BF16))
                    for i in range(2)]
            llb = [es2.enter_context(nc.sbuf_tensor(f"llb{i}", [128, TP], BF16))
                   for i in range(2)]
            wrow = [es2.enter_context(nc.sbuf_tensor(f"wrow{i}", [1, NE * TP], BF16))
                    for i in range(2)]
            sin = es2.enter_context(nc.sbuf_tensor("sin", [128, KC * TP], F32))
            xin = es2.enter_context(nc.sbuf_tensor("xin", [128, 2 * DM], F32))
            hw = es2.enter_context(nc.sbuf_tensor("hw", [128, 32 * TP], BF16))
            basebf = es2.enter_context(nc.sbuf_tensor("basebf", [128, 2048], BF16))
            rbf = es2.enter_context(nc.sbuf_tensor("rbf", [128, 2048], BF16))
            vbf = es2.enter_context(nc.sbuf_tensor("vbf", [128, 1024], BF16))
            wrep = es2.enter_context(nc.sbuf_tensor("wrep", [128, NE * TP], BF16))
            sgbf = es2.enter_context(nc.sbuf_tensor("sgbf", [128, KC * TP], BF16))
            sgtm = es2.enter_context(nc.sbuf_tensor("sgtm", [128, 2 * DM], BF16))
            otm = es2.enter_context(nc.sbuf_tensor("otm", [128, DM], F32))
            flsh = es2.enter_context(nc.sbuf_tensor("flsh", [128, 1], F32))

            p.dma("pool", lambda e: e.dma_start(
                out=WiT.rearrange("p (kc m) -> p kc m", m=4096), in_=wit_d.rearrange("(kc p) m -> p kc m", p=128)),
                writes=["WiT"], stream="w2", barrier=True)
            p.dma("pool", lambda e: e.dma_start(
                out=WoT.rearrange("p (kc m) -> p kc m", m=1024), in_=wot_d.rearrange("(kc p) m -> p kc m", p=128)),
                writes=["WoT"], stream="w2", barrier=True)
            p.dma("pool", lambda e: e.dma_start(out=Bstk[:], in_=bstk_d),
                  writes=["Bstk"], stream="w2", barrier=True)

            for sub in range(NSUB):
                par = sub % 2
                xnkey = ("xnf2", par)
                p.dma("sync", lambda e, sub=sub, par=par: e.dma_start(
                    out=xnf2[par].rearrange("p (kc t) -> p kc t", t=TP),
                    in_=xnT_d.rearrange("(kc p) s -> p kc s", p=128)
                    [:, :, sub * TP:(sub + 1) * TP]), writes=[xnkey], stream="xn2")
                p.dma("sync", lambda e, sub=sub, par=par: e.dma_start(
                    out=llb[par][:], in_=llT_d[:, sub * TP:(sub + 1) * TP]),
                    writes=[("llb", par)], stream=f"llr{par}")
                p.dma("pool", lambda e, sub=sub, par=par: e.dma_start(
                    out=wrow[par].rearrange("o (e t) -> o e t", t=TP),
                    in_=w_scr_d[:, sub * TP:(sub + 1) * TP].unsqueeze(0)),
                    writes=[("wrow", par)], stream=f"wr{par}")
                p.dma("pool", lambda e, sub=sub: e.dma_start(
                    out=sin.rearrange("p (mc t) -> p mc t", t=TP),
                    in_=sT_d.rearrange("(mc p) s -> p mc s", p=128)
                    [:, :, sub * TP:(sub + 1) * TP]), writes=["sin"], stream="sir")
                p.dma("sync", lambda e, sub=sub: e.dma_start(
                    out=xin.rearrange("p (tb d) -> p tb d", d=DM),
                    in_=x_d[sub * TP:(sub + 1) * TP, :]
                    .rearrange("(tb p) d -> p tb d", p=128)), writes=["xin"], stream="xir")
                for ei in range(NE):
                    p.op("pe", lambda e, ei=ei, par=par: e.matmul(
                        PT[1][:, ei * TP:(ei + 1) * TP], onesb[:],
                        wrow[par][0:1, ei * TP:(ei + 1) * TP],
                        start=True, stop=True),
                        reads=["onesb", ("wrow", par)],
                        writes=[BK(1, 0) if ei < 2 else BK(1, 1)])
                p.op("dve", lambda e: e.tensor_copy(wrep[:], PT[1][:, 0:NE * TP]),
                     reads=[BK(1, 0), BK(1, 1)], writes=["wrep"])
                if debug and DBG_P2 and sub == 0:
                    p.dma("pool", lambda e: e.dma_start(
                        out=p2_d[:, 0:1024], in_=wrep[:]), reads=["wrep"],
                        stream="dbg")
                for gp in range(4):
                  for g in (2 * gp, 2 * gp + 1):
                    bi = 0 if g % 2 == 0 else 2
                    bb = basebf[:, (g % 2) * 1024:(g % 2 + 1) * 1024]
                    for fcq in range(4):
                        fc = g * 4 + fcq
                        for kc in range(KC):
                            p.op("pe", lambda e, fc=fc, fcq=fcq, kc=kc, par=par,
                                 bi=bi: e.matmul(
                                PT[bi][:, fcq * TP:(fcq + 1) * TP],
                                WiT[:, kc * 4096 + fc * 128:
                                    kc * 4096 + (fc + 1) * 128],
                                xnf2[par][:, kc * TP:(kc + 1) * TP],
                                start=(kc == 0), stop=(kc == KC - 1)),
                                reads=["WiT", xnkey],
                                writes=[BK(bi, 0) if fcq < 2 else BK(bi, 1)])
                    p.op("dve", lambda e, bi=bi, bb=bb: e.tensor_copy(bb, PT[bi][:]),
                         reads=[BK(bi, 0), BK(bi, 1)], writes=[("basebf", g % 2)])
                  for g in (2 * gp, 2 * gp + 1):
                    bb = basebf[:, (g % 2) * 1024:(g % 2 + 1) * 1024]
                    for ei in range(NE):
                        ui = 1 if ei % 2 == 0 else 3
                        tp = (96, 0) if ei == 3 else None
                        for fcq in range(4):
                            fc = g * 4 + fcq
                            p.op("pe", lambda e, ei=ei, fc=fc, fcq=fcq, par=par,
                                 tp=tp, ui=ui: e.matmul(
                                PT[ui][:, fcq * TP:(fcq + 1) * TP],
                                Bstk[32 * ei:32 * ei + 32, fc * 128:(fc + 1) * 128],
                                llb[par][32 * ei:32 * ei + 32, :],
                                start=True, stop=False, tile_position=tp),
                                reads=["Bstk", ("llb", par)],
                                writes=[BK(ui, 0) if fcq < 2 else BK(ui, 1)])
                            p.op("pe", lambda e, fcq=fcq, ui=ui, bb=bb: e.matmul(
                                PT[ui][:, fcq * TP:(fcq + 1) * TP], idb[:],
                                bb[:, fcq * TP:(fcq + 1) * TP],
                                start=False, stop=True),
                                reads=["idb", ("basebf", g % 2)],
                                writes=[BK(ui, 0) if fcq < 2 else BK(ui, 1)])
                        rv = rbf[:, (ei % 2) * 1024:(ei % 2 + 1) * 1024]
                        if ei % 2 == 0:
                            p.op("act", lambda e, ui=ui, rv=rv: e.activation(
                                rv, PT[ui][:], AF.Relu),
                                reads=[BK(ui, 0), BK(ui, 1)],
                                writes=[("rbf", ei % 2)])
                        else:
                            p.op("dve", lambda e, ui=ui, rv=rv: e.tensor_scalar(
                                rv, PT[ui][:], 0.0, None, op0=ALU.max),
                                reads=[BK(ui, 0), BK(ui, 1)],
                                writes=[("rbf", ei % 2)])
                        wrap = wrep[:, ei * TP:(ei + 1) * TP].unsqueeze(1)\
                            .broadcast_to([128, 4, TP])
                        if ei == 0:
                            p.op("dve", lambda e, g=g, wrap=wrap, rv=rv:
                                 e.tensor_tensor(
                                hw[:, g * 1024:(g + 1) * 1024]
                                .rearrange("p (q t) -> p q t", t=TP),
                                rv.rearrange("p (q t) -> p q t", t=TP), wrap,
                                op=ALU.mult),
                                reads=[("rbf", ei % 2), "wrep"], writes=[("hw", g)])
                        else:
                            p.op("dve", lambda e, wrap=wrap, rv=rv: e.tensor_tensor(
                                vbf.rearrange("p (q t) -> p q t", t=TP),
                                rv.rearrange("p (q t) -> p q t", t=TP), wrap,
                                op=ALU.mult),
                                reads=[("rbf", ei % 2), "wrep"], writes=["vbf"])
                            eng_acc = "pool" if ei == 2 else "dve"
                            p.op(eng_acc, lambda e, g=g: e.tensor_tensor(
                                hw[:, g * 1024:(g + 1) * 1024],
                                hw[:, g * 1024:(g + 1) * 1024], vbf[:],
                                op=ALU.add),
                                reads=[("hw", g), "vbf"], writes=[("hw", g)])
                for mch in range(2):
                    pps = PT[2] if mch == 0 else PT[3]
                    bks = [BK(2, 0), BK(2, 1)] if mch == 0 else [BK(3, 0), BK(3, 1)]
                    for mcq in range(4):
                        mc = mch * 4 + mcq
                        for fc in range(32):
                            p.op("pe", lambda e, mc=mc, mcq=mcq, fc=fc, pps=pps:
                                 e.matmul(
                                pps[:, mcq * TP:(mcq + 1) * TP],
                                WoT[:, fc * 1024 + mc * 128:
                                    fc * 1024 + (mc + 1) * 128],
                                hw[:, fc * TP:(fc + 1) * TP],
                                start=(fc == 0), stop=(fc == 31)),
                                reads=["WoT"] + ([("hw", gg) for gg in range(8)]
                                                 if fc in (0, 31) else []),
                                writes=[bks[0] if mcq < 2 else bks[1]])
                    p.op("act", lambda e, mch=mch, pps=pps: e.activation(
                        rbf[:, mch * 1024:(mch + 1) * 1024], pps[:], AF.Silu),
                        reads=bks, writes=[("rbf", mch)])
                if debug and DBG_P2 and sub == 0:
                    p.dma("pool", lambda e: e.dma_start(
                        out=p2_d[:, 2048:10240], in_=hw[:]),
                        reads=[("hw", g) for g in range(8)], stream="dbg")
                    p.dma("pool", lambda e: e.dma_start(
                        out=p2_d[:, 10240:12288], in_=gbf[:]),
                        reads=[("gbf", 0), ("gbf", 1)], stream="dbg")
                p.op("dve", lambda e: e.tensor_tensor(
                    sgbf[:], sin[:], rbf[:], op=ALU.mult),
                    reads=["sin", ("rbf", 0), ("rbf", 1)], writes=["sgbf"])
                if debug and DBG_P2 and sub == 0:
                    p.dma("pool", lambda e: e.dma_start(
                        out=p2_d[:, 12288:14336], in_=sgbf[:]),
                        reads=["sgbf"], stream="dbg")
                for mc in range(KC):
                    for tb in range(2):
                        p.op("pe", lambda e, mc=mc, tb=tb: e.matmul(
                            PT[tb][:, mc * 128:(mc + 1) * 128],
                            sgbf[:, mc * TP + tb * 128:mc * TP + (tb + 1) * 128],
                            idb[:], start=True, stop=True),
                            reads=["sgbf", "idb"],
                            writes=[BK(tb, 0) if mc < 4 else BK(tb, 1)])
                for tb in range(2):
                    p.op("dve", lambda e, tb=tb: e.tensor_copy(
                        sgtm[:, tb * DM:(tb + 1) * DM], PT[tb][:]),
                        reads=[BK(tb, 0), BK(tb, 1)], writes=[("sgtm", tb)])
                if debug and DBG_P2 and sub == 0:
                    p.dma("pool", lambda e: e.dma_start(
                        out=p2_d[:, 14336:16384], in_=sgtm[:]),
                        reads=[("sgtm", 0), ("sgtm", 1)], stream="dbg")
                for tb in range(2):
                    p.op("dve", lambda e, tb=tb: e.tensor_tensor(
                        otm[:], sgtm[:, tb * DM:(tb + 1) * DM],
                        xin[:, tb * DM:(tb + 1) * DM], op=ALU.add),
                        reads=[("sgtm", tb), "xin"], writes=["otm"])
                    p.dma("sync", lambda e, sub=sub, tb=tb: e.dma_start(
                        out=out_d[sub * TP + tb * 128:sub * TP + (tb + 1) * 128, :],
                        in_=otm[:]), reads=["otm"], stream="outw")
            p.fence()
            p.flush(blk2, final=True)
    return nc


_NC_CACHE = {}


def kernel(L0_attn_x, L0_attn_W_in, L0_attn_W_xp, L0_attn_W_B, L0_attn_W_C,
           L0_attn_W_out, L0_attn_D, L0_attn_A_log, L0_ffn_Wg, L0_ffn_bg,
           L0_ffn_Wi, L0_ffn_Wo, L0_ffn_A, L0_ffn_B):
    f32 = np.float32
    x = np.ascontiguousarray(np.asarray(L0_attn_x, f32))
    W_in = np.ascontiguousarray(np.asarray(L0_attn_W_in, f32))
    W_xp = np.ascontiguousarray(np.asarray(L0_attn_W_xp, f32))
    W_B = np.ascontiguousarray(np.asarray(L0_attn_W_B, f32))
    W_C = np.ascontiguousarray(np.asarray(L0_attn_W_C, f32))
    W_out = np.ascontiguousarray(np.asarray(L0_attn_W_out, f32))
    Dv = np.ascontiguousarray(np.asarray(L0_attn_D, f32).reshape(1, DI))
    A_log = np.ascontiguousarray(np.asarray(L0_attn_A_log, f32).reshape(1, 1))
    Wg = np.asarray(L0_ffn_Wg, f32)
    bg = np.ascontiguousarray(np.asarray(L0_ffn_bg, f32).reshape(1, NE))
    Wi = np.asarray(L0_ffn_Wi, f32)
    Wo = np.asarray(L0_ffn_Wo, f32)
    Am = np.asarray(L0_ffn_A, f32)
    Bm = np.asarray(L0_ffn_B, f32)

    W_inTl = np.ascontiguousarray(W_in[:, :DI].T)
    WgT = np.ascontiguousarray(Wg.T)
    WiT = np.ascontiguousarray(Wi.T)
    WoT = np.ascontiguousarray(Wo.T)
    A_pad = np.zeros((128, DM), f32)
    B_stk = np.zeros((128, FF), f32)
    for e in range(NE):
        for r in range(4):
            A_pad[32 * e + r] = Am[e, r]
            B_stk[32 * e + r] = Bm[e, :, r]
    A_padT = np.ascontiguousarray(A_pad.T)

    key = "v1"
    if key not in _NC_CACHE:
        _NC_CACHE[key] = build_program()
    nc = _NC_CACHE[key]

    shared = dict(w_in=W_in, w_intl=W_inTl, w_xp=W_xp, w_b=W_B, w_c=W_C,
                  w_out=W_out, dvec=Dv, a_log=A_log, wg_t=WgT, bg=bg,
                  wi_t=WiT, wo_t=WoT, a_pad=A_padT, b_stk=B_stk)
    in_maps = [dict(x=np.ascontiguousarray(x[c]), **shared) for c in range(8)]
    res = run_bass_kernel_spmd(nc, in_maps, core_ids=list(range(8)))
    return np.stack([res.results[c]["out"] for c in range(8)], axis=0)


# version 09 (final)
# speedup vs baseline: 1.1130x; 1.0094x over previous
import contextlib
import numpy as np
import ml_dtypes
import concourse.bass as bass
import concourse.mybir as mybir
from concourse.bass_utils import run_bass_kernel_spmd

F32 = mybir.dt.float32
BF16 = mybir.dt.bfloat16
AF = mybir.ActivationFunctionType
ALU = mybir.AluOpType
AX = mybir.AxisListType

S = 4096
DM = 1024
DI = 2048
DS = 64
FF = 4096
NE = 4
KC = DM // 128
MCI = DI // 128
TP = 256
NSUB = S // TP
PB = 128
DBG_LN = False
DBG_P2 = True
NPB = S // PB

ENGINES = ("sync", "pe", "dve", "act", "pool")
BLKNAME = {"sync": "sync", "pe": "tensor", "dve": "vector", "act": "scalar",
           "pool": "gpsimd"}


class Sched:

    def __init__(self, serial=False):
        self.cnt = {e: 0 for e in ENGINES}
        self.water = {e: {} for e in ENGINES}
        self.serial = serial
        self._last = None
        self._seg = None
        self.sems = None
        self.barrier = set()
        self.streams = []
        self.last_write = {}
        self.readers = {}
        self.ro = set()
        self.mystart = {}

    def begin(self):
        self._seg = {e: [] for e in ENGINES}

    def _add(self, eng, key, fn, deps, inc):
        deps = [d for d in deps if d is not None]
        if self.serial and self._last is not None:
            deps = deps + [self._last]
        self._seg[eng].append((fn, deps, key, inc))
        self.cnt[key] = self.cnt.get(key, 0) + inc
        lab = (key, self.cnt[key])
        if inc > 0:
            self._last = lab
        return lab

    def _emit(self, eng, key, fn, reads, writes, deps, inc):
        deps = list(deps)
        for r in reads:
            w = self.last_write.get(r)
            if w is not None:
                deps.append(w)
        for wk in writes:
            w = self.last_write.get(wk)
            if w is not None:
                deps.append(w)
            deps.extend(self.readers.get(wk, ()))
        lab = self._add(eng, key, fn, deps, inc)
        for r in reads:
            if r not in self.ro:
                self.readers.setdefault(r, []).append(lab)
        for wk in writes:
            self.last_write[wk] = lab
            self.readers[wk] = []
        return lab

    def op(self, eng, fn, reads=(), writes=(), deps=()):
        return self._emit(eng, eng, fn, reads, writes, deps, 1)

    def op2(self, eng, fn, reads=(), writes=(), deps=()):
        self._emit(eng, eng, fn, reads, writes, deps, 1)
        return self._emit(eng, eng, fn, reads, writes, deps, 1)

    def dma(self, eng, fn, reads=(), writes=(), deps=(), stream="misc",
            barrier=False):
        key = "dma:" + stream
        if key not in self.cnt:
            self.streams.append(key)
            if barrier:
                self.barrier.add(key)
            self.sems[key] = self.es.enter_context(
                self.nc.semaphore("sem_" + stream))
        return self._emit(eng, key, fn, reads, writes, deps, 16)

    def fence(self):
        cuts = {k: self.cnt[k] for k in list(self.cnt)}
        for e in ENGINES:
            deps = [(k, c) for k, c in cuts.items() if k != e and c > 0]
            self._seg[e].append((None, deps, e, 0))

    def flush(self, blk, final=False):
        seg = self._seg
        self._seg = None
        for ename in ENGINES:
            ops = seg[ename]
            if not ops and not (final and ename == "sync"):
                continue

            def body(eng, ename=ename, ops=ops):
                water = self.water[ename]
                mycnt = self.mystart.get(ename, 0)
                for fn, deps, key, inc in ops:
                    best = {}
                    for (k2, c2) in deps:
                        if k2 == ename:
                            if ename != "pe" and c2 > mycnt - 8:
                                if c2 > best.get(k2, 0):
                                    best[k2] = c2
                            continue
                        if k2 in self.barrier:
                            c2 = self.cnt[k2]
                        if c2 > best.get(k2, 0):
                            best[k2] = c2
                    for k2, c2 in best.items():
                        if c2 > water.get(k2, 0):
                            eng.wait_ge(self.sems[k2], c2)
                            water[k2] = c2
                    if fn is None:
                        continue
                    inst = fn(eng)
                    if inc > 0:
                        inst.then_inc(self.sems[key], inc)
                        if key == ename:
                            mycnt += 1
                self.mystart[ename] = mycnt
                if final and ename == "sync":
                    for k in self.streams:
                        eng.wait_ge(self.sems[k], self.cnt[k])
            getattr(blk, BLKNAME[ename])(body)


def build_program(serial=False, debug=False):
    nc = bass.Bass("TRN2", target_bir_lowering=False)
    bfnp = ml_dtypes.bfloat16

    def din(name, shape):
        return nc.dram_tensor(name, shape, F32, kind="ExternalInput").ap()

    x_d = din("x", [S, DM])
    win_d = din("w_in", [DM, 2 * DI])
    wintl_d = din("w_intl", [DI, DM])
    wxp_d = din("w_xp", [DI, 129])
    wb_d = din("w_b", [DS, DI])
    wc_d = din("w_c", [DS, DI])
    wout_d = din("w_out", [DI, DM])
    dvec_d = din("dvec", [1, DI])
    alog_d = din("a_log", [1, 1])
    wgt_d = din("wg_t", [DM, NE])
    bg_d = din("bg", [1, NE])
    wit_d = din("wi_t", [DM, FF])
    wot_d = din("wo_t", [FF, DM])
    apad_d = din("a_pad", [DM, 128])
    bstk_d = din("b_stk", [128, FF])
    out_d = nc.dram_tensor("out", [S, DM], F32, kind="ExternalOutput").ap()

    scrkind = "ExternalOutput" if debug else "Internal"
    if debug:
        xnbf_dbg_d = nc.dram_tensor("xnbf_dbg", [128, 2 * DM], F32, kind="ExternalOutput").ap()
        rstd_dbg_d = nc.dram_tensor("rstd_dbg", [128, 2 * NPB], F32, kind="ExternalOutput").ap()
        gdbg_d = nc.dram_tensor("gdbg", [128, 32], F32, kind="ExternalOutput").ap()
        ln_dbg_d = nc.dram_tensor("ln_dbg", [128, 4 * NPB], F32, kind="ExternalOutput").ap()
        p2_d = nc.dram_tensor("p2dbg", [128, 16384], F32, kind="ExternalOutput").ap()
    xnT_d = nc.dram_tensor("xnT", [DM, S], BF16, kind=scrkind).ap()
    sT_d = nc.dram_tensor("sT", [DM, S], F32, kind=scrkind).ap()
    llT_d = nc.dram_tensor("llTs", [128, S], BF16, kind=scrkind).ap()
    a_scr_d = nc.dram_tensor("a_scr", [1, S], F32, kind=scrkind).ap()
    w_scr_d = nc.dram_tensor("w_scr", [NE, S], F32, kind=scrkind).ap()

    idb_d = nc.inline_tensor(np.eye(128, dtype=bfnp), name="c_idb").ap()
    idf_d = nc.inline_tensor(np.eye(128, dtype=np.float32), name="c_idf").ap()
    onesf_d = nc.inline_tensor(np.ones((1, 128), np.float32), name="c_onesf").ap()
    onesb_d = nc.inline_tensor(np.ones((1, 128), bfnp), name="c_onesb").ap()

    p = Sched(serial=serial)
    p.nc = nc

    def atl(setid):
        def f(e):
            return e.add_instruction(mybir.InstLoadActFuncSet(
                name=nc.get_next_instruction_name(),
                act_func_set_id=setid, ins=[], outs=[]))
        return f
    p.ro.update(["idb", "idf", "onesf", "onesb", "W_in", "W_out", "WBC",
                 "WproBC", "grhs", "Apad", "WinTl", "Wxp", "WiT", "WoT",
                 "Bstk", "dcol", "acol", "bgrep", "x_d"])

    es = contextlib.ExitStack()
    with es:
        p.es = es
        sems = {e: es.enter_context(nc.semaphore("sem_" + e)) for e in ENGINES}
        p.sems = sems
        idb = es.enter_context(nc.sbuf_tensor("idb", [128, 128], BF16))
        idf = es.enter_context(nc.sbuf_tensor("idf", [128, 128], F32))
        onesf = es.enter_context(nc.sbuf_tensor("onesf", [1, 128], F32))
        onesb = es.enter_context(nc.sbuf_tensor("onesb", [1, 128], BF16))
        dcol = es.enter_context(nc.sbuf_tensor("dcol", [128, MCI], F32))
        acol = es.enter_context(nc.sbuf_tensor("acol", [128, 1], F32))
        bgrep = es.enter_context(nc.sbuf_tensor("bgrep", [128, NE], F32))
        PT = [es.enter_context(nc.psum_tensor(f"pt{i}", [128, 1024], F32))
              for i in range(4)]
        BK = lambda i, h: ("bank", i, h)

        p.begin()
        p.op("act", atl(6))
        p.dma("sync", lambda e: e.dma_start(out=idb[:], in_=idb_d), writes=["idb"], stream="cst", barrier=True)
        p.dma("sync", lambda e: e.dma_start(out=idf[:], in_=idf_d), writes=["idf"], stream="cst", barrier=True)
        p.dma("sync", lambda e: e.dma_start(out=onesf[:], in_=onesf_d), writes=["onesf"], stream="cst", barrier=True)
        p.dma("sync", lambda e: e.dma_start(out=onesb[:], in_=onesb_d), writes=["onesb"], stream="cst", barrier=True)

        with contextlib.ExitStack() as es01:
            W_in = es01.enter_context(nc.sbuf_tensor("W_in", [128, KC * 4096], BF16))
            W_out = es01.enter_context(nc.sbuf_tensor("W_out", [128, MCI * 1024], BF16))
            WBC = es01.enter_context(nc.sbuf_tensor("WBC", [128, DI], BF16))
            WproBC = es01.enter_context(nc.sbuf_tensor("WproBC", [128, KC * 128], BF16))
            grhs = es01.enter_context(nc.sbuf_tensor("grhs", [128, KC * 5], BF16))
            Apad = es01.enter_context(nc.sbuf_tensor("Apad", [128, KC * 128], BF16))

            p.dma("pool", lambda e: e.dma_start(
                out=W_in.rearrange("p (kc m) -> p kc m", m=4096), in_=win_d.rearrange("(kc p) m -> p kc m", p=128)),
                writes=["W_in"], stream="wini", barrier=True)
            p.dma("pool", lambda e: e.dma_start(
                out=W_out.rearrange("p (kc m) -> p kc m", m=1024), in_=wout_d.rearrange("(kc p) m -> p kc m", p=128)),
                writes=["W_out"], stream="wini", barrier=True)
            p.dma("pool", lambda e: e.dma_start(out=WBC[0:64, :], in_=wb_d),
                  writes=[("WBC", 0)], stream="wini", barrier=True)
            p.dma("pool", lambda e: e.dma_start(out=WBC[64:128, :], in_=wc_d),
                  writes=[("WBC", 1)], stream="wini", barrier=True)
            l_wgt = p.dma("pool", lambda e: e.dma_start(
                out=grhs.rearrange("p (kc n) -> p kc n", n=5)[:, :, 1:5],
                in_=wgt_d.rearrange("(kc p) e -> p kc e", p=128)), stream="wini", barrier=True)
            p.dma("pool", lambda e: e.dma_start(
                out=Apad.rearrange("p (kc m) -> p kc m", m=128), in_=apad_d.rearrange("(kc p) m -> p kc m", p=128)),
                writes=["Apad"], stream="wini", barrier=True)

            with contextlib.ExitStack() as es0, nc.Block() as blk0:
                WinTl = es0.enter_context(nc.sbuf_tensor("WinTl", [128, 16 * 1024], BF16))
                Wxp = es0.enter_context(nc.sbuf_tensor("Wxp", [128, 16 * 129], BF16))
                dflat = es0.enter_context(nc.sbuf_tensor("dflat", [1, DI], F32))
                alog = es0.enter_context(nc.sbuf_tensor("alog", [1, 1], F32))
                a1n = es0.enter_context(nc.sbuf_tensor("a1n", [1, 1], F32))
                bgs = es0.enter_context(nc.sbuf_tensor("bgs", [1, NE], F32))
                xbuf = [es0.enter_context(nc.sbuf_tensor(f"xbuf{i}", [128, DM], F32))
                        for i in range(2)]
                xnbf = es0.enter_context(nc.sbuf_tensor("xnbf", [128, DM], BF16))
                xnfm = es0.enter_context(nc.sbuf_tensor("xnfm", [128, DM], BF16))
                llbf = es0.enter_context(nc.sbuf_tensor("llbf", [128, 128], BF16))
                scrA = es0.enter_context(nc.sbuf_tensor("scrA", [128, DM], F32))
                scrB = es0.enter_context(nc.sbuf_tensor("scrB", [128, DM], F32))
                cols = {}
                for nm in ["mcol", "msum", "sqcol", "m2t", "negb", "lnv", "rstd", "df",
                           "rd", "ad", "ed", "ld_", "spd", "logad", "acb",
                           "mx1", "negmx", "sume", "rs1", "m1c", "m2c"]:
                    cols[nm] = es0.enter_context(nc.sbuf_tensor(nm, [128, 1], F32))
                q4 = {}
                for nm in ["lg4", "ex4", "sc4", "mask1", "s24", "keepm", "w4t"]:
                    q4[nm] = es0.enter_context(nc.sbuf_tensor(nm, [128, NE], F32))
                artmp = es0.enter_context(nc.sbuf_tensor("artmp", [1, 128], F32))
                wrtmp4 = es0.enter_context(nc.sbuf_tensor("wrtmp4", [4, 128], F32))
                dall = es0.enter_context(nc.sbuf_tensor("dall", [128, NPB], F32))
                lgall = es0.enter_context(nc.sbuf_tensor("lgall", [128, 4 * NPB], F32))
                rdall = es0.enter_context(nc.sbuf_tensor("rdall", [128, NPB], F32))
                adall = es0.enter_context(nc.sbuf_tensor("adall", [128, NPB], F32))
                eadall = es0.enter_context(nc.sbuf_tensor("eadall", [128, NPB], F32))
                ladall = es0.enter_context(nc.sbuf_tensor("ladall", [128, NPB], F32))
                spall = es0.enter_context(nc.sbuf_tensor("spall", [128, NPB], F32))
                logall = es0.enter_context(nc.sbuf_tensor("logall", [128, NPB], F32))
                aall = es0.enter_context(nc.sbuf_tensor("aall", [128, NPB], F32))
                aTall = es0.enter_context(nc.sbuf_tensor("aTall", [32, 128], F32))
                mxall = es0.enter_context(nc.sbuf_tensor("mxall", [128, NPB], F32))
                exs = es0.enter_context(nc.sbuf_tensor("exs", [128, 4 * NPB], F32))
                exall = es0.enter_context(nc.sbuf_tensor("exall", [128, 4 * NPB], F32))
                sumall = es0.enter_context(nc.sbuf_tensor("sumall", [128, NPB], F32))
                rsall = es0.enter_context(nc.sbuf_tensor("rsall", [128, NPB], F32))
                scall = es0.enter_context(nc.sbuf_tensor("scall", [128, 4 * NPB], F32))
                s24t = es0.enter_context(nc.sbuf_tensor("s24t", [128, 4 * NPB], F32))
                wTall = es0.enter_context(nc.sbuf_tensor("wTall", [32, 128], F32))

                p.dma("pool", lambda e: e.dma_start(
                    out=WinTl.rearrange("p (kc m) -> p kc m", m=1024), in_=wintl_d.rearrange("(kc p) m -> p kc m", p=128)),
                    writes=["WinTl"], stream="wini", barrier=True)
                p.dma("pool", lambda e: e.dma_start(
                    out=Wxp.rearrange("p (kc m) -> p kc m", m=129), in_=wxp_d.rearrange("(kc p) m -> p kc m", p=128)),
                    writes=["Wxp"], stream="wini", barrier=True)
                p.dma("sync", lambda e: e.dma_start(out=dflat[:], in_=dvec_d),
                      writes=["dflat"], stream="cst", barrier=True)
                p.dma("sync", lambda e: e.dma_start(out=alog[:], in_=alog_d),
                      writes=["alog"], stream="cst", barrier=True)
                p.dma("sync", lambda e: e.dma_start(out=bgs[:], in_=bg_d),
                      writes=["bgs"], stream="cst", barrier=True)

                p.op2("act", lambda e: e.activation(a1n[:], alog[:], AF.Exp),
                     reads=["alog"], writes=["a1n"])
                p.op("pe", lambda e: e.matmul(PT[2][:, 0:1], onesf[:], a1n[:],
                                              start=True, stop=True),
                     reads=["onesf", "a1n"], writes=[BK(2, 0)])
                p.op2("dve", lambda e: e.tensor_copy(acol[:], PT[2][:, 0:1]),
                     reads=[BK(2, 0)], writes=["acol"])
                p.op("pe", lambda e: e.matmul(PT[2][:, 512:516], onesf[:], bgs[:],
                                              start=True, stop=True),
                     reads=["onesf", "bgs"], writes=[BK(2, 1)])
                p.op2("dve", lambda e: e.tensor_copy(bgrep[:], PT[2][:, 512:516]),
                     reads=[BK(2, 1)], writes=["bgrep"])
                for mc in range(MCI):
                    p.op("pe", lambda e, mc=mc: e.matmul(
                        PT[3][:, mc:mc + 1],
                        dflat[0:1, mc * 128:(mc + 1) * 128], onesf[0:1, 0:1],
                        start=True, stop=True),
                        reads=["dflat", "onesf"], writes=[BK(3, 0)])
                p.op2("dve", lambda e: e.tensor_copy(dcol[:], PT[3][:, 0:MCI]),
                     reads=[BK(3, 0)], writes=["dcol"])
                if debug:
                    p.dma("sync", lambda e: e.dma_start(
                        out=gdbg_d[:, 0:1], in_=acol[:],
                        allow_slow_non_contiguous=True), reads=["acol"], stream="dbg")
                for mc in range(KC):
                    lm = None
                    for kc2 in range(16):
                        lm = p.op("pe", lambda e, mc=mc, kc2=kc2: e.matmul(
                            PT[3][:, 512:641],
                            WinTl[:, kc2 * 1024 + mc * 128: kc2 * 1024 + (mc + 1) * 128],
                            Wxp[:, kc2 * 129:(kc2 + 1) * 129],
                            start=(kc2 == 0), stop=(kc2 == 15)),
                            reads=["WinTl", "Wxp"], writes=[BK(3, 1)])
                    p.op("dve", lambda e, mc=mc: e.tensor_copy(
                        grhs[:, mc * 5: mc * 5 + 1], PT[3][:, 512:513]),
                        reads=[BK(3, 1)], writes=[("grhs0", mc)], deps=[l_wgt])
                    p.op("dve", lambda e, mc=mc: e.tensor_copy(
                        WproBC[:, mc * 128:(mc + 1) * 128], PT[3][:, 513:641]),
                        reads=[BK(3, 1)], writes=[("WproBC", mc)])

                for blk in range(NPB):
                    par = blk % 2
                    xb = xbuf[par]
                    xkey = ("xbuf", par)
                    p.dma("sync", lambda e, blk=blk, xb=xb: e.dma_start(
                        out=xb[:], in_=x_d[blk * PB:(blk + 1) * PB, :]),
                        writes=[xkey], stream=f"xpro{par}")
                    p.op2("dve", lambda e, xb=xb: e.reduce_sum(
                        cols["msum"][:], xb[:], axis=AX.X),
                        reads=[xkey], writes=["msum"])
                    p.op2("dve", lambda e: e.tensor_scalar(
                        cols["mcol"][:], cols["msum"][:], 1.0 / DM, None,
                        op0=ALU.mult), reads=["msum"], writes=["mcol"])
                    p.op("act", lambda e, xb=xb: e.activation(
                        scrB[:], xb[:], AF.Square, accum_out=cols["sqcol"][:]),
                        reads=[xkey], writes=["scrB", "sqcol"])
                    p.op2("dve", lambda e: e.tensor_tensor(
                        cols["m2t"][:], cols["mcol"][:], cols["mcol"][:],
                        op=ALU.mult), reads=["mcol"], writes=["m2t"])
                    p.op2("dve", lambda e: e.tensor_scalar(
                        cols["negb"][:], cols["m2t"][:], -1.0, 1e-5,
                        op0=ALU.mult, op1=ALU.add), reads=["m2t"], writes=["negb"])
                    p.op2("act", lambda e: e.activation(
                        cols["lnv"][:], cols["sqcol"][:], AF.Ln,
                        bias=cols["negb"][:], scale=1.0 / DM),
                        reads=["sqcol", "negb"], writes=["lnv"])
                    p.op2("act", lambda e: e.activation(
                        cols["rstd"][:], cols["lnv"][:], AF.Exp, scale=-0.5),
                        reads=["lnv"], writes=["rstd"])
                    p.op("dve", lambda e, xb=xb: e.tensor_scalar(
                        xnbf[:], xb[:], cols["mcol"][:], cols["rstd"][:],
                        op0=ALU.subtract, op1=ALU.mult),
                        reads=[xkey, "mcol", "rstd"], writes=["xnbf"])
                    if debug and DBG_LN:
                        for ci, key in [(0, "sqcol"), (1, "negb"), (2, "lnv"),
                                        (3, "rstd")]:
                            p.dma("sync", lambda e, blk=blk, ci=ci, key=key:
                                  e.dma_start(
                                out=ln_dbg_d[:, 4 * blk + ci:4 * blk + ci + 1],
                                in_=cols[key][:],
                                allow_slow_non_contiguous=True),
                                reads=[key], stream="dbg")
                        p.dma("sync", lambda e, blk=blk: e.dma_start(
                            out=rstd_dbg_d[:, 2 * blk:2 * blk + 1],
                            in_=cols["rstd"][:],
                            allow_slow_non_contiguous=True), reads=["rstd"], stream="dbg")
                        p.dma("sync", lambda e, blk=blk: e.dma_start(
                            out=rstd_dbg_d[:, 2 * blk + 1:2 * blk + 2],
                            in_=cols["mcol"][:],
                            allow_slow_non_contiguous=True), reads=["mcol"], stream="dbg")
                        if blk < 2:
                            p.dma("gpsimd" if False else "pool",
                                  lambda e, blk=blk: e.dma_start(
                                out=xnbf_dbg_d[:, blk * DM:(blk + 1) * DM],
                                in_=xnbf[:]), reads=["xnbf"], stream="dbg")
                    for kc in range(KC):
                        p.op("pe", lambda e, kc=kc: e.matmul(
                            PT[0][:, kc * 128:(kc + 1) * 128],
                            xnbf[:, kc * 128:(kc + 1) * 128], idb[:],
                            start=True, stop=True),
                            reads=["xnbf", "idb"],
                            writes=[BK(0, 0) if kc < 4 else BK(0, 1)])
                    p.op("dve", lambda e: e.tensor_copy(xnfm[:], PT[0][:, 0:DM]),
                         reads=[BK(0, 0), BK(0, 1)], writes=["xnfm"])
                    p.dma("sync", lambda e, blk=blk: e.dma_start(
                        out=xnT_d.rearrange("(kc p) s -> p kc s", p=128)
                        [:, :, blk * PB:(blk + 1) * PB],
                        in_=xnfm.rearrange("p (kc t) -> p kc t", t=128)),
                        reads=["xnfm"], stream="xnTw")
                    for kc in range(KC):
                        p.op("pe", lambda e, kc=kc: e.matmul(
                            PT[1][:, 0:5], xnfm[:, kc * 128:(kc + 1) * 128],
                            grhs[:, kc * 5:(kc + 1) * 5],
                            start=(kc == 0), stop=(kc == KC - 1)),
                            reads=["xnfm"] + [("grhs0", kc2) for kc2 in range(KC)],
                            writes=[BK(1, 0)])
                    for kc in range(KC):
                        p.op("pe", lambda e, kc=kc: e.matmul(
                            PT[1][:, 512:640], Apad[:, kc * 128:(kc + 1) * 128],
                            xnfm[:, kc * 128:(kc + 1) * 128],
                            start=(kc == 0), stop=(kc == KC - 1)),
                            reads=["xnfm", "Apad"],
                            writes=[BK(1, 1)])
                    p.op("act", lambda e: e.activation(
                        llbf[:], PT[1][:, 512:640], AF.Identity),
                        reads=[BK(1, 1)], writes=["llbf"])
                    p.dma("sync", lambda e, blk=blk: e.dma_start(
                        out=llT_d[:, blk * PB:(blk + 1) * PB], in_=llbf[:]),
                        reads=["llbf"], stream="llw")
                    p.op2("dve", lambda e, blk=blk: e.tensor_copy(
                        dall[:, blk:blk + 1], PT[1][:, 0:1]),
                        reads=[BK(1, 0)], writes=[("dall", blk)])
                    p.op2("dve", lambda e, blk=blk: e.tensor_tensor(
                        lgall[:, blk * 4:(blk + 1) * 4], PT[1][:, 1:5],
                        bgrep[:], op=ALU.add),
                        reads=[BK(1, 0), "bgrep"], writes=[("lgall", blk)])
                dR = [("dall", b) for b in range(NPB)]
                lR = [("lgall", b) for b in range(NPB)]
                p.op2("act", lambda e: e.activation(rdall[:], dall[:], AF.Relu),
                      reads=dR, writes=["rdall"])
                p.op2("act", lambda e: e.activation(adall[:], dall[:], AF.Abs),
                      reads=dR, writes=["adall"])
                p.op2("act", lambda e: e.activation(eadall[:], adall[:], AF.Exp,
                                                    scale=-1.0),
                      reads=["adall"], writes=["eadall"])
                p.op2("act", lambda e: e.activation(ladall[:], eadall[:], AF.Ln,
                                                    bias=1.0),
                      reads=["eadall"], writes=["ladall"])
                p.op2("dve", lambda e: e.tensor_tensor(
                    spall[:], rdall[:], ladall[:], op=ALU.add),
                    reads=["rdall", "ladall"], writes=["spall"])
                p.op2("dve", lambda e: e.tensor_scalar(
                    logall[:], spall[:], acol[:], None, op0=ALU.mult),
                    reads=["spall", "acol"], writes=["logall"])
                p.op2("act", lambda e: e.activation(aall[:], logall[:], AF.Exp,
                                                    scale=-1.0),
                      reads=["logall"], writes=["aall"])
                p.op("pe", lambda e: e.matmul(PT[2][0:32, 0:128], aall[:], idf[:],
                                              start=True, stop=True),
                     reads=["aall", "idf"], writes=[BK(2, 0)])
                p.op2("dve", lambda e: e.tensor_copy(aTall[:], PT[2][0:32, 0:128]),
                      reads=[BK(2, 0)], writes=["aTall"])
                p.dma("sync", lambda e: e.dma_start(
                    out=a_scr_d.rearrange("o (b t) -> b o t", t=128),
                    in_=aTall[:].unsqueeze(1)),
                    reads=["aTall"], stream="aw")
                lgv = lgall.rearrange("p (b e) -> p b e", e=4)
                def bc(t2d):
                    return t2d[:].unsqueeze(2).broadcast_to([128, NPB, 4])
                def v3(t):
                    return t.rearrange("p (b e) -> p b e", e=4)
                p.op2("dve", lambda e: e.reduce_max(
                    mxall[:].unsqueeze(2), lgv, axis=AX.X),
                    reads=lR, writes=["mxall"])
                p.op2("dve", lambda e: e.tensor_tensor(
                    v3(exs), lgv, bc(mxall), op=ALU.subtract),
                    reads=lR + ["mxall"], writes=["exsub"])
                p.op2("act", lambda e: e.activation(exall[:], exs[:], AF.Exp),
                      reads=["exsub"], writes=["exall"])
                p.op2("dve", lambda e: e.reduce_sum(
                    sumall[:].unsqueeze(2), v3(exall), axis=AX.X),
                    reads=["exall"], writes=["sumall"])
                p.op2("dve", lambda e: e.reciprocal(rsall[:], sumall[:]),
                      reads=["sumall"], writes=["rsall"])
                p.op2("dve", lambda e: e.tensor_tensor(
                    v3(scall), v3(exall), bc(rsall), op=ALU.mult),
                    reads=["exall", "rsall"], writes=["scall"])
                p.op2("dve", lambda e: e.reduce_max(
                    mxall[:].unsqueeze(2), v3(scall), axis=AX.X),
                    reads=["scall"], writes=["m1all"])
                p.op2("dve", lambda e: e.tensor_tensor(
                    v3(exs), v3(scall), bc(mxall), op=ALU.is_ge),
                    reads=["scall", "m1all"], writes=["mask1all"])
                p.op2("dve", lambda e: e.scalar_tensor_tensor(
                    s24t[:], exs[:], -1e30, scall[:],
                    op0=ALU.mult, op1=ALU.add),
                    reads=["mask1all", "scall"], writes=["s24all"])
                p.op2("dve", lambda e: e.reduce_max(
                    mxall[:].unsqueeze(2), v3(s24t), axis=AX.X),
                    reads=["s24all"], writes=["m2all"])
                p.op2("dve", lambda e: e.tensor_tensor(
                    v3(exs), v3(scall), bc(mxall), op=ALU.is_ge),
                    reads=["scall", "m2all"], writes=["keepall"])
                p.op2("dve", lambda e: e.tensor_tensor(
                    s24t[:], scall[:], exs[:], op=ALU.mult),
                    reads=["scall", "keepall"], writes=["w4tall"])
                for ei in range(NE):
                    p.op("pe", lambda e, ei=ei: e.matmul(
                        PT[2][0:32, 512:640],
                        s24t.rearrange("p (b e) -> p b e", e=4)[:, :, ei], idf[:],
                        start=True, stop=True),
                        reads=["w4tall", "idf"], writes=[BK(2, 1)])
                    p.op2("dve", lambda e: e.tensor_copy(
                        wTall[0:32, :], PT[2][0:32, 512:640]),
                        reads=[BK(2, 1)], writes=["wTall"])
                    p.dma("sync", lambda e, ei=ei: e.dma_start(
                        out=w_scr_d[ei:ei + 1, :].rearrange("o (b t) -> b o t", t=128),
                        in_=wTall[0:32, :].unsqueeze(1)),
                        reads=["wTall"], stream="ww")
                p.fence()
                p.flush(blk0)

            p.begin()
            p.op("act", atl(18))
            with contextlib.ExitStack() as es1, nc.Block() as blk1:
                xnf = [es1.enter_context(nc.sbuf_tensor(f"xnf{i}", [128, KC * TP], BF16))
                       for i in range(2)]
                arow = [es1.enter_context(nc.sbuf_tensor(f"arow{i}", [1, TP], F32))
                        for i in range(2)]
                xzbf = es1.enter_context(nc.sbuf_tensor("xzbf", [128, 32 * TP], BF16))
                bcraw = es1.enter_context(nc.sbuf_tensor("bcraw", [128, TP], BF16))
                bbf = es1.enter_context(nc.sbuf_tensor("bbf", [128, MCI * TP], BF16))
                cbf = es1.enter_context(nc.sbuf_tensor("cbf", [128, MCI * TP], BF16))
                arep = es1.enter_context(nc.sbuf_tensor("arep", [128, TP], F32))
                hch = [es1.enter_context(nc.sbuf_tensor(f"hch{i}", [128, TP], F32))
                       for i in range(2)]
                hcar = es1.enter_context(nc.sbuf_tensor("hcar", [128, MCI], F32))
                t1b = [es1.enter_context(nc.sbuf_tensor(f"t1b{i}", [128, TP], F32))
                       for i in range(2)]
                ypre = es1.enter_context(nc.sbuf_tensor("ypre", [128, MCI * TP], F32))
                szbf = es1.enter_context(nc.sbuf_tensor("szbf", [128, MCI * TP], BF16))
                ybf = es1.enter_context(nc.sbuf_tensor("ybf", [128, MCI * TP], BF16))
                ssb = es1.enter_context(nc.sbuf_tensor("ssb", [128, KC * TP], F32))

                for sub in range(NSUB):
                    par = sub % 2
                    xnkey = ("xnf", par)
                    p.dma("sync", lambda e, sub=sub, par=par: e.dma_start(
                        out=xnf[par].rearrange("p (kc t) -> p kc t", t=TP),
                        in_=xnT_d.rearrange("(kc p) s -> p kc s", p=128)
                        [:, :, sub * TP:(sub + 1) * TP]), writes=[xnkey], stream=f"xn1_{par}")
                    p.dma("sync", lambda e, sub=sub, par=par: e.dma_start(
                        out=arow[par][:], in_=a_scr_d[:, sub * TP:(sub + 1) * TP]),
                        writes=[("arow", par)], stream=f"ar{par}")
                    for mcp in range(16):
                        pz = PT[0][:, 0:512] if mcp % 2 == 0 else PT[0][:, 512:1024]
                        bkey = BK(0, mcp % 2)
                        for j in range(2):
                            mc = 2 * mcp + j
                            for kc in range(KC):
                                p.op("pe", lambda e, mc=mc, kc=kc, pz=pz, j=j,
                                     par=par: e.matmul(
                                    pz[:, j * TP:(j + 1) * TP],
                                    W_in[:, kc * 4096 + mc * 128:
                                         kc * 4096 + (mc + 1) * 128],
                                    xnf[par][:, kc * TP:(kc + 1) * TP],
                                    start=(kc == 0), stop=(kc == KC - 1)),
                                    reads=["W_in", xnkey],
                                    writes=[bkey])
                        eng = "dve" if mcp % 2 == 0 else "act"
                        if eng == "dve":
                            p.op(eng, lambda e, mcp=mcp, pz=pz: e.tensor_copy(
                                xzbf[:, mcp * 512:(mcp + 1) * 512], pz[:]),
                                reads=[bkey], writes=[("xz", 2 * mcp), ("xz", 2 * mcp + 1)])
                        else:
                            p.op(eng, lambda e, mcp=mcp, pz=pz: e.activation(
                                xzbf[:, mcp * 512:(mcp + 1) * 512], pz[:],
                                AF.Identity),
                                reads=[bkey], writes=[("xz", 2 * mcp), ("xz", 2 * mcp + 1)])
                    for kc in range(KC):
                        p.op("pe", lambda e, kc=kc, par=par: e.matmul(
                            PT[1][:, 0:TP], WproBC[:, kc * 128:(kc + 1) * 128],
                            xnf[par][:, kc * TP:(kc + 1) * TP],
                            start=(kc == 0), stop=(kc == KC - 1)),
                            reads=[xnkey] + [("WproBC", k) for k in range(KC)],
                            writes=[BK(1, 0)])
                    p.op("dve", lambda e: e.tensor_copy(bcraw[:], PT[1][:, 0:TP]),
                         reads=[BK(1, 0)], writes=["bcraw"])
                    p.op("pe", lambda e, par=par: e.matmul(
                        PT[1][:, 0:TP], onesf[:], arow[par][:],
                        start=True, stop=True),
                        reads=["onesf", ("arow", par)], writes=[BK(1, 0)])
                    p.op("dve", lambda e: e.tensor_copy(arep[:], PT[1][:, 0:TP]),
                         reads=[BK(1, 0)], writes=["arep"])
                    brot = [(PT[1][:, 512:768], BK(1, 1)), (PT[2][:, 0:256], BK(2, 0)),
                            (PT[0][:, 0:256], BK(0, 0)), (PT[0][:, 512:768], BK(0, 1))]
                    crot = [(PT[2][:, 512:768], BK(2, 1)), (PT[3][:, 0:256], BK(3, 0)),
                            (PT[3][:, 512:768], BK(3, 1)), (PT[1][:, 0:256], BK(1, 0))]
                    for mc in range(MCI):
                        pb, bkey = brot[mc % 4]
                        p.op("pe", lambda e, mc=mc, pb=pb: e.matmul(
                            pb[:, 0:TP], WBC[0:64, mc * 128:(mc + 1) * 128],
                            bcraw[0:64, :], start=True, stop=True),
                            reads=[("WBC", 0), "bcraw"], writes=[bkey])
                        pc, ckey = crot[mc % 4]
                        p.op("pe", lambda e, mc=mc, pc=pc: e.matmul(
                            pc[:, 0:TP], WBC[64:128, mc * 128:(mc + 1) * 128],
                            bcraw[64:128, :], start=True, stop=True),
                            reads=[("WBC", 1), "bcraw"], writes=[ckey])
                        p.op("dve", lambda e, mc=mc, pb=pb: e.tensor_tensor(
                            bbf[:, mc * TP:(mc + 1) * TP], pb[:, 0:TP],
                            xzbf[:, mc * TP:(mc + 1) * TP], op=ALU.mult),
                            reads=[bkey, ("xz", mc)], writes=[("bbf", mc)])
                        p.op("act", lambda e, mc=mc, pc=pc: e.activation(
                            cbf[:, mc * TP:(mc + 1) * TP], pc[:, 0:TP],
                            AF.Identity), reads=[ckey], writes=[("cbf", mc)])
                    for mc in range(MCI):
                        hb = hch[mc % 2]
                        tb = t1b[mc % 2]
                        init = 0.0 if sub == 0 else hcar[:, mc:mc + 1]
                        p.op("dve", lambda e, mc=mc, hb=hb, init=init:
                             e.tensor_tensor_scan(
                            hb[:], arep[:], bbf[:, mc * TP:(mc + 1) * TP], init,
                            op0=ALU.mult, op1=ALU.add),
                            reads=["arep", ("bbf", mc), ("hcar", mc)],
                            writes=[("hch", mc % 2)])
                        p.op("dve", lambda e, mc=mc, hb=hb: e.tensor_copy(
                            hcar[:, mc:mc + 1], hb[:, TP - 1:TP]),
                            reads=[("hch", mc % 2)], writes=[("hcar", mc)])
                        p.op("dve", lambda e, mc=mc, hb=hb, tb=tb: e.tensor_tensor(
                            tb[:], cbf[:, mc * TP:(mc + 1) * TP], hb[:],
                            op=ALU.mult),
                            reads=[("cbf", mc), ("hch", mc % 2)],
                            writes=[("t1b", mc % 2)])
                        p.op("dve", lambda e, mc=mc, tb=tb: e.scalar_tensor_tensor(
                            ypre[:, mc * TP:(mc + 1) * TP],
                            xzbf[:, mc * TP:(mc + 1) * TP], dcol[:, mc:mc + 1],
                            tb[:], op0=ALU.mult, op1=ALU.add),
                            reads=[("xz", mc), "dcol", ("t1b", mc % 2)],
                            writes=[("ypre", mc)])
                    p.op("act", lambda e: e.activation(
                        szbf[:], xzbf[:, MCI * TP:32 * TP], AF.Silu),
                        reads=[("xz", mc) for mc in range(MCI, 32)],
                        writes=["szbf"])
                    p.op("dve", lambda e: e.tensor_tensor(
                        ybf[:], ypre[:], szbf[:], op=ALU.mult),
                        reads=[("ypre", mc) for mc in range(MCI)] + ["szbf"],
                        writes=["ybf"])
                    for mc2 in range(KC):
                        ps_, skey = crot[mc2 % 4]
                        for kc2 in range(MCI):
                            p.op("pe", lambda e, mc2=mc2, kc2=kc2, ps_=ps_:
                                 e.matmul(
                                ps_[:, 0:TP],
                                W_out[:, kc2 * 1024 + mc2 * 128:
                                      kc2 * 1024 + (mc2 + 1) * 128],
                                ybf[:, kc2 * TP:(kc2 + 1) * TP],
                                start=(kc2 == 0), stop=(kc2 == MCI - 1)),
                                reads=["W_out", "ybf"],
                                writes=[skey])
                        p.op("dve", lambda e, mc2=mc2, ps_=ps_: e.tensor_copy(
                            ssb[:, mc2 * TP:(mc2 + 1) * TP], ps_[:, 0:TP]),
                            reads=[skey], writes=[("ssb", mc2)])
                    p.dma("sync", lambda e, sub=sub: e.dma_start(
                        out=sT_d.rearrange("(mc p) s -> p mc s", p=128)
                        [:, :, sub * TP:(sub + 1) * TP],
                        in_=ssb.rearrange("p (mc t) -> p mc t", t=TP)),
                        reads=[("ssb", mc2) for mc2 in range(KC)], stream="sw")
                p.fence()
                p.flush(blk1)

        p.begin()
        with contextlib.ExitStack() as es2, nc.Block() as blk2:
            WiT = es2.enter_context(nc.sbuf_tensor("WiT", [128, KC * 4096], BF16))
            WoT = es2.enter_context(nc.sbuf_tensor("WoT", [128, 32 * 1024], BF16))
            Bstk = es2.enter_context(nc.sbuf_tensor("Bstk", [128, FF], BF16))
            xnf2 = [es2.enter_context(nc.sbuf_tensor(f"xnf2_{i}", [128, KC * TP], BF16))
                    for i in range(2)]
            llb = [es2.enter_context(nc.sbuf_tensor(f"llb{i}", [128, TP], BF16))
                   for i in range(2)]
            wrow = [es2.enter_context(nc.sbuf_tensor(f"wrow{i}", [1, NE * TP], BF16))
                    for i in range(2)]
            sin = es2.enter_context(nc.sbuf_tensor("sin", [128, KC * TP], F32))
            xin = es2.enter_context(nc.sbuf_tensor("xin", [128, 2 * DM], F32))
            hw = es2.enter_context(nc.sbuf_tensor("hw", [128, 32 * TP], BF16))
            basebf = es2.enter_context(nc.sbuf_tensor("basebf", [128, 2048], BF16))
            rbf = es2.enter_context(nc.sbuf_tensor("rbf", [128, 2048], BF16))
            vbf = es2.enter_context(nc.sbuf_tensor("vbf", [128, 1024], BF16))
            wrep = es2.enter_context(nc.sbuf_tensor("wrep", [128, NE * TP], BF16))
            sgbf = es2.enter_context(nc.sbuf_tensor("sgbf", [128, KC * TP], BF16))
            sgtm = es2.enter_context(nc.sbuf_tensor("sgtm", [128, 2 * DM], BF16))
            otm = es2.enter_context(nc.sbuf_tensor("otm", [128, DM], F32))
            flsh = es2.enter_context(nc.sbuf_tensor("flsh", [128, 1], F32))

            p.dma("pool", lambda e: e.dma_start(
                out=WiT.rearrange("p (kc m) -> p kc m", m=4096), in_=wit_d.rearrange("(kc p) m -> p kc m", p=128)),
                writes=["WiT"], stream="w2", barrier=True)
            p.dma("pool", lambda e: e.dma_start(
                out=WoT.rearrange("p (kc m) -> p kc m", m=1024), in_=wot_d.rearrange("(kc p) m -> p kc m", p=128)),
                writes=["WoT"], stream="w2", barrier=True)
            p.dma("pool", lambda e: e.dma_start(out=Bstk[:], in_=bstk_d),
                  writes=["Bstk"], stream="w2", barrier=True)

            for sub in range(NSUB):
                par = sub % 2
                xnkey = ("xnf2", par)
                p.dma("sync", lambda e, sub=sub, par=par: e.dma_start(
                    out=xnf2[par].rearrange("p (kc t) -> p kc t", t=TP),
                    in_=xnT_d.rearrange("(kc p) s -> p kc s", p=128)
                    [:, :, sub * TP:(sub + 1) * TP]), writes=[xnkey], stream="xn2")
                p.dma("sync", lambda e, sub=sub, par=par: e.dma_start(
                    out=llb[par][:], in_=llT_d[:, sub * TP:(sub + 1) * TP]),
                    writes=[("llb", par)], stream=f"llr{par}")
                p.dma("pool", lambda e, sub=sub, par=par: e.dma_start(
                    out=wrow[par].rearrange("o (e t) -> o e t", t=TP),
                    in_=w_scr_d[:, sub * TP:(sub + 1) * TP].unsqueeze(0)),
                    writes=[("wrow", par)], stream=f"wr{par}")
                p.dma("pool", lambda e, sub=sub: e.dma_start(
                    out=sin.rearrange("p (mc t) -> p mc t", t=TP),
                    in_=sT_d.rearrange("(mc p) s -> p mc s", p=128)
                    [:, :, sub * TP:(sub + 1) * TP]), writes=["sin"], stream="sir")
                p.dma("sync", lambda e, sub=sub: e.dma_start(
                    out=xin.rearrange("p (tb d) -> p tb d", d=DM),
                    in_=x_d[sub * TP:(sub + 1) * TP, :]
                    .rearrange("(tb p) d -> p tb d", p=128)), writes=["xin"], stream="xir")
                for ei in range(NE):
                    p.op("pe", lambda e, ei=ei, par=par: e.matmul(
                        PT[1][:, ei * TP:(ei + 1) * TP], onesb[:],
                        wrow[par][0:1, ei * TP:(ei + 1) * TP],
                        start=True, stop=True),
                        reads=["onesb", ("wrow", par)],
                        writes=[BK(1, 0) if ei < 2 else BK(1, 1)])
                p.op("dve", lambda e: e.tensor_copy(wrep[:], PT[1][:, 0:NE * TP]),
                     reads=[BK(1, 0), BK(1, 1)], writes=["wrep"])
                if debug and DBG_P2 and sub == 0:
                    p.dma("pool", lambda e: e.dma_start(
                        out=p2_d[:, 0:1024], in_=wrep[:]), reads=["wrep"],
                        stream="dbg")
                for gp in range(4):
                  for g in (2 * gp, 2 * gp + 1):
                    bi = 0 if g % 2 == 0 else 2
                    bb = basebf[:, (g % 2) * 1024:(g % 2 + 1) * 1024]
                    for fcq in range(4):
                        fc = g * 4 + fcq
                        for kc in range(KC):
                            p.op("pe", lambda e, fc=fc, fcq=fcq, kc=kc, par=par,
                                 bi=bi: e.matmul(
                                PT[bi][:, fcq * TP:(fcq + 1) * TP],
                                WiT[:, kc * 4096 + fc * 128:
                                    kc * 4096 + (fc + 1) * 128],
                                xnf2[par][:, kc * TP:(kc + 1) * TP],
                                start=(kc == 0), stop=(kc == KC - 1)),
                                reads=["WiT", xnkey],
                                writes=[BK(bi, 0) if fcq < 2 else BK(bi, 1)])
                    p.op("dve", lambda e, bi=bi, bb=bb: e.tensor_copy(bb, PT[bi][:]),
                         reads=[BK(bi, 0), BK(bi, 1)], writes=[("basebf", g % 2)])
                  for g in (2 * gp, 2 * gp + 1):
                    bb = basebf[:, (g % 2) * 1024:(g % 2 + 1) * 1024]
                    for ei in range(NE):
                        ui = 1 if ei % 2 == 0 else 3
                        tp = (96, 0) if ei == 3 else None
                        for fcq in range(4):
                            fc = g * 4 + fcq
                            p.op("pe", lambda e, ei=ei, fc=fc, fcq=fcq, par=par,
                                 tp=tp, ui=ui: e.matmul(
                                PT[ui][:, fcq * TP:(fcq + 1) * TP],
                                Bstk[32 * ei:32 * ei + 32, fc * 128:(fc + 1) * 128],
                                llb[par][32 * ei:32 * ei + 32, :],
                                start=True, stop=False, tile_position=tp),
                                reads=["Bstk", ("llb", par)],
                                writes=[BK(ui, 0) if fcq < 2 else BK(ui, 1)])
                            p.op("pe", lambda e, fcq=fcq, ui=ui, bb=bb: e.matmul(
                                PT[ui][:, fcq * TP:(fcq + 1) * TP], idb[:],
                                bb[:, fcq * TP:(fcq + 1) * TP],
                                start=False, stop=True),
                                reads=["idb", ("basebf", g % 2)],
                                writes=[BK(ui, 0) if fcq < 2 else BK(ui, 1)])
                        rv = rbf[:, (ei % 2) * 1024:(ei % 2 + 1) * 1024]
                        if ei % 2 == 0:
                            p.op("act", lambda e, ui=ui, rv=rv: e.activation(
                                rv, PT[ui][:], AF.Relu),
                                reads=[BK(ui, 0), BK(ui, 1)],
                                writes=[("rbf", ei % 2)])
                        else:
                            p.op("dve", lambda e, ui=ui, rv=rv: e.tensor_scalar(
                                rv, PT[ui][:], 0.0, None, op0=ALU.max),
                                reads=[BK(ui, 0), BK(ui, 1)],
                                writes=[("rbf", ei % 2)])
                        wrap = wrep[:, ei * TP:(ei + 1) * TP].unsqueeze(1)\
                            .broadcast_to([128, 4, TP])
                        if ei == 0:
                            p.op("dve", lambda e, g=g, wrap=wrap, rv=rv:
                                 e.tensor_tensor(
                                hw[:, g * 1024:(g + 1) * 1024]
                                .rearrange("p (q t) -> p q t", t=TP),
                                rv.rearrange("p (q t) -> p q t", t=TP), wrap,
                                op=ALU.mult),
                                reads=[("rbf", ei % 2), "wrep"], writes=[("hw", g)])
                        else:
                            p.op("dve", lambda e, wrap=wrap, rv=rv: e.tensor_tensor(
                                vbf.rearrange("p (q t) -> p q t", t=TP),
                                rv.rearrange("p (q t) -> p q t", t=TP), wrap,
                                op=ALU.mult),
                                reads=[("rbf", ei % 2), "wrep"], writes=["vbf"])
                            eng_acc = "pool" if ei == 2 else "dve"
                            p.op(eng_acc, lambda e, g=g: e.tensor_tensor(
                                hw[:, g * 1024:(g + 1) * 1024],
                                hw[:, g * 1024:(g + 1) * 1024], vbf[:],
                                op=ALU.add),
                                reads=[("hw", g), "vbf"], writes=[("hw", g)])
                for mch in range(2):
                    pps = PT[2] if mch == 0 else PT[3]
                    bks = [BK(2, 0), BK(2, 1)] if mch == 0 else [BK(3, 0), BK(3, 1)]
                    for mcq in range(4):
                        mc = mch * 4 + mcq
                        for fc in range(32):
                            p.op("pe", lambda e, mc=mc, mcq=mcq, fc=fc, pps=pps:
                                 e.matmul(
                                pps[:, mcq * TP:(mcq + 1) * TP],
                                WoT[:, fc * 1024 + mc * 128:
                                    fc * 1024 + (mc + 1) * 128],
                                hw[:, fc * TP:(fc + 1) * TP],
                                start=(fc == 0), stop=(fc == 31)),
                                reads=["WoT"] + ([("hw", gg) for gg in range(8)]
                                                 if fc in (0, 31) else []),
                                writes=[bks[0] if mcq < 2 else bks[1]])
                    p.op("act", lambda e, mch=mch, pps=pps: e.activation(
                        rbf[:, mch * 1024:(mch + 1) * 1024], pps[:], AF.Silu),
                        reads=bks, writes=[("rbf", mch)])
                if debug and DBG_P2 and sub == 0:
                    p.dma("pool", lambda e: e.dma_start(
                        out=p2_d[:, 2048:10240], in_=hw[:]),
                        reads=[("hw", g) for g in range(8)], stream="dbg")
                    p.dma("pool", lambda e: e.dma_start(
                        out=p2_d[:, 10240:12288], in_=gbf[:]),
                        reads=[("gbf", 0), ("gbf", 1)], stream="dbg")
                p.op("dve", lambda e: e.tensor_tensor(
                    sgbf[:], sin[:], rbf[:], op=ALU.mult),
                    reads=["sin", ("rbf", 0), ("rbf", 1)], writes=["sgbf"])
                if debug and DBG_P2 and sub == 0:
                    p.dma("pool", lambda e: e.dma_start(
                        out=p2_d[:, 12288:14336], in_=sgbf[:]),
                        reads=["sgbf"], stream="dbg")
                for mc in range(KC):
                    for tb in range(2):
                        p.op("pe", lambda e, mc=mc, tb=tb: e.matmul(
                            PT[tb][:, mc * 128:(mc + 1) * 128],
                            sgbf[:, mc * TP + tb * 128:mc * TP + (tb + 1) * 128],
                            idb[:], start=True, stop=True),
                            reads=["sgbf", "idb"],
                            writes=[BK(tb, 0) if mc < 4 else BK(tb, 1)])
                for tb in range(2):
                    p.op("dve", lambda e, tb=tb: e.tensor_copy(
                        sgtm[:, tb * DM:(tb + 1) * DM], PT[tb][:]),
                        reads=[BK(tb, 0), BK(tb, 1)], writes=[("sgtm", tb)])
                if debug and DBG_P2 and sub == 0:
                    p.dma("pool", lambda e: e.dma_start(
                        out=p2_d[:, 14336:16384], in_=sgtm[:]),
                        reads=[("sgtm", 0), ("sgtm", 1)], stream="dbg")
                for tb in range(2):
                    p.op("dve", lambda e, tb=tb: e.tensor_tensor(
                        otm[:], sgtm[:, tb * DM:(tb + 1) * DM],
                        xin[:, tb * DM:(tb + 1) * DM], op=ALU.add),
                        reads=[("sgtm", tb), "xin"], writes=["otm"])
                    p.dma("sync", lambda e, sub=sub, tb=tb: e.dma_start(
                        out=out_d[sub * TP + tb * 128:sub * TP + (tb + 1) * 128, :],
                        in_=otm[:]), reads=["otm"], stream="outw")
            p.fence()
            p.flush(blk2, final=True)
    return nc


_NC_CACHE = {}


def kernel(L0_attn_x, L0_attn_W_in, L0_attn_W_xp, L0_attn_W_B, L0_attn_W_C,
           L0_attn_W_out, L0_attn_D, L0_attn_A_log, L0_ffn_Wg, L0_ffn_bg,
           L0_ffn_Wi, L0_ffn_Wo, L0_ffn_A, L0_ffn_B):
    f32 = np.float32
    x = np.ascontiguousarray(np.asarray(L0_attn_x, f32))
    W_in = np.ascontiguousarray(np.asarray(L0_attn_W_in, f32))
    W_xp = np.ascontiguousarray(np.asarray(L0_attn_W_xp, f32))
    W_B = np.ascontiguousarray(np.asarray(L0_attn_W_B, f32))
    W_C = np.ascontiguousarray(np.asarray(L0_attn_W_C, f32))
    W_out = np.ascontiguousarray(np.asarray(L0_attn_W_out, f32))
    Dv = np.ascontiguousarray(np.asarray(L0_attn_D, f32).reshape(1, DI))
    A_log = np.ascontiguousarray(np.asarray(L0_attn_A_log, f32).reshape(1, 1))
    Wg = np.asarray(L0_ffn_Wg, f32)
    bg = np.ascontiguousarray(np.asarray(L0_ffn_bg, f32).reshape(1, NE))
    Wi = np.asarray(L0_ffn_Wi, f32)
    Wo = np.asarray(L0_ffn_Wo, f32)
    Am = np.asarray(L0_ffn_A, f32)
    Bm = np.asarray(L0_ffn_B, f32)

    W_inTl = np.ascontiguousarray(W_in[:, :DI].T)
    WgT = np.ascontiguousarray(Wg.T)
    WiT = np.ascontiguousarray(Wi.T)
    WoT = np.ascontiguousarray(Wo.T)
    A_pad = np.zeros((128, DM), f32)
    B_stk = np.zeros((128, FF), f32)
    for e in range(NE):
        for r in range(4):
            A_pad[32 * e + r] = Am[e, r]
            B_stk[32 * e + r] = Bm[e, :, r]
    A_padT = np.ascontiguousarray(A_pad.T)

    key = "v1"
    if key not in _NC_CACHE:
        _NC_CACHE[key] = build_program()
    nc = _NC_CACHE[key]

    shared = dict(w_in=W_in, w_intl=W_inTl, w_xp=W_xp, w_b=W_B, w_c=W_C,
                  w_out=W_out, dvec=Dv, a_log=A_log, wg_t=WgT, bg=bg,
                  wi_t=WiT, wo_t=WoT, a_pad=A_padT, b_stk=B_stk)
    in_maps = [dict(x=np.ascontiguousarray(x[c]), **shared) for c in range(8)]
    res = run_bass_kernel_spmd(nc, in_maps, core_ids=list(range(8)))
    return np.stack([res.results[c]["out"] for c in range(8)], axis=0)
